# version 04
# baseline (speedup 1.0000x reference)
import numpy as np
from contextlib import ExitStack

import concourse.bass as bass
import concourse.mybir as mybir
from concourse.bass import AP
from concourse.bass_utils import run_bass_kernel_spmd

F32 = mybir.dt.float32
BF16 = mybir.dt.bfloat16
I32 = mybir.dt.int32
AF = mybir.ActivationFunctionType
ALU = mybir.AluOpType
AX = mybir.AxisListType

B = 2
L = 2048
D = 1024
H = 4
DK = DV = 256
NT = L // 128
NTT = B * NT
BT = B * L
NCH = D // 128

ENGS = ("sync", "tensor", "vector", "scalar", "gpsimd")


class Sem:
    __slots__ = ("h", "n")

    def __init__(self, h):
        self.h = h
        self.n = 0


class Prog:
    def __init__(self, nc, stack):
        self.nc = nc
        self.stack = stack
        self.q = {e: [] for e in ENGS}
        self._id = 0
        self.bar_sem = self.sem("bar")

    def sem(self, name=None):
        self._id += 1
        h = self.stack.enter_context(self.nc.semaphore(name or f"s{self._id}"))
        return Sem(h)

    def sb(self, name, shape, dt, scope=None):
        return (scope or self.stack).enter_context(
            self.nc.sbuf_tensor(name, list(shape), dt))

    def ps(self, name, shape, dt=F32, scope=None):
        return (scope or self.stack).enter_context(
            self.nc.psum_tensor(name, list(shape), dt))

    def op(self, eng, fn, waits=(), inc=None, k=1):
        ws = tuple((s.h, int(v)) for s, v in waits if s is not None)
        tgt = None
        ih = None
        if inc is not None:
            inc.n += k
            tgt = inc.n
            ih = inc.h

        def thunk(e, fn=fn, ws=ws, ih=ih, k=k):
            for h, v in ws:
                e.wait_ge(h, v)
            ins = fn(e)
            if ih is not None:
                assert ins is not None, "op with inc must return an instruction"
                ins.then_inc(ih, k)
        self.q[eng].append(thunk)
        return tgt

    def drain(self, sem):
        self.q["sync"].append(lambda e, h=sem.h, v=sem.n: e.wait_ge(h, v))

    def barrier(self):
        s = self.bar_sem
        s.n += 5
        tgt = s.n
        for e in ENGS:
            def thunk(eng, tgt=tgt, h=s.h):
                eng.sem_inc(h, 1)
                eng.wait_ge(h, tgt)
            self.q[e].append(thunk)
        return tgt

    def build(self):
        nc = self.nc
        with nc.Block() as block:
            for name in ENGS:
                thunks = self.q[name]
                if not thunks:
                    continue

                def body(e, ths=tuple(thunks)):
                    for t in ths:
                        t(e)
                getattr(block, name)(body)


class Ctx:
    def __init__(self, nc, prog):
        self.nc = nc
        self.p = prog
        self.c = {}
        self.d = {}
        self._cf = {}

    def constf(self, v, np_=128):
        return self._cf[float(v)][0:np_, :]

    def make_constf(self, values):
        p = self.p
        for v in values:
            v = float(v)
            if v in self._cf:
                continue
            t = p.sb(f"cf_{len(self._cf)}", [128, 1], F32)
            p.op("gpsimd", lambda e, t=t, v=v: e.memset(t[:], v))
            self._cf[v] = t


def fmv(dram_ap):
    return [dram_ap[c] for c in range(NCH)]


def emit_consts(ctx):
    p = ctx.p
    ident_bf = p.sb("ident_bf", [128, 128], BF16)
    ident_f = p.sb("ident_f", [128, 128], F32)
    mask_sl = p.sb("mask_sl", [128, 128], BF16)
    mask_ui = p.sb("mask_ui", [128, 128], BF16)
    mask_su = p.sb("mask_su", [128, 128], BF16)
    ones_col = p.sb("ones_col", [1, 128], BF16)
    done = p.sem("consts")

    def build(g):
        g.memset(ident_bf[:], 1.0)
        g.affine_select(out=ident_bf[:], in_=ident_bf[:], compare_op=ALU.is_equal,
                        fill=0.0, base=0, pattern=[[-1, 128]], channel_multiplier=1)
        g.memset(ident_f[:], 1.0)
        g.affine_select(out=ident_f[:], in_=ident_f[:], compare_op=ALU.is_equal,
                        fill=0.0, base=0, pattern=[[-1, 128]], channel_multiplier=1)
        g.memset(mask_sl[:], 1.0)
        g.affine_select(out=mask_sl[:], in_=mask_sl[:], compare_op=ALU.is_gt,
                        fill=0.0, base=0, pattern=[[-1, 128]], channel_multiplier=1)
        g.memset(mask_ui[:], 1.0)
        g.affine_select(out=mask_ui[:], in_=mask_ui[:], compare_op=ALU.is_ge,
                        fill=0.0, base=0, pattern=[[1, 128]], channel_multiplier=-1)
        g.memset(mask_su[:], 1.0)
        g.affine_select(out=mask_su[:], in_=mask_su[:], compare_op=ALU.is_gt,
                        fill=0.0, base=0, pattern=[[1, 128]], channel_multiplier=-1)
        return g.memset(ones_col[:], 1.0)
    p.op("gpsimd", build, inc=done)
    ctx.c.update(ident_bf=ident_bf, ident_f=ident_f, mask_sl=mask_sl,
                 mask_ui=mask_ui, mask_su=mask_su, ones_col=ones_col)
    ctx.make_constf([1e-5, 1e-12, 0.0, 1.0, 1e-3])
    p.barrier()


def emit_ln0(ctx, x_in, xnFM, m1_part):
    p = ctx.p
    rstd = p.sb("ln_rstd", [128, NTT], F32)
    negmr = p.sb("ln_negmr", [128, NTT], F32)
    xa = p.sb("xa", [128, NCH * BT], BF16)
    S = ExitStack()

    xbuf = p.sb("ln_x", [128, NTT * D], F32, scope=S)
    sm = p.sb("ln_sm", [128, NTT], F32, scope=S)
    sq = p.sb("ln_sq", [128, NTT], F32, scope=S)
    tmp = p.sb("ln_tmp", [128, NTT], F32, scope=S)
    junk = p.sb("ln_junk", [128, D], F32, scope=S)
    mx = p.sb("ln_mx", [128, 1], F32, scope=S)
    mxt = p.sb("ln_mxt", [128, 1], F32, scope=S)
    xnt = p.sb("ln_xnt", [128, D], F32, scope=S)
    xnt2 = p.sb("ln_xnt2", [128, D], F32, scope=S)
    m1row = p.sb("ln_m1row", [1, 128], F32, scope=S)
    m1o = p.sb("ln_m1o", [1, 1], F32, scope=S)
    pst = p.ps("ln_pst", [128, 128], F32, scope=S)
    mxD = ctx.d["scr128"]

    ld = p.sem()
    vx = p.sem()
    sc = p.sem()
    tp = p.sem()
    st = p.sem()

    x_t = x_in.rearrange("b (jt p) d -> (b jt) p d", p=128)
    for i in range(NTT):
        p.op("sync", lambda e, i=i: e.dma_start(out=xbuf[:, i * D:(i + 1) * D], in_=x_t[i]),
             inc=ld, k=16)
    for i in range(NTT):
        s_i = p.op("scalar", lambda e, i=i: e.activation(
            junk[:, 0:D], xbuf[:, i * D:(i + 1) * D], AF.Square),
            waits=[(ld, 16 * (i + 1)), (vx, vx.n)], inc=sc)
        p.op("vector", lambda e, i=i: e.reduce_sum(
            sq[:, i:i + 1], junk[:, 0:D], axis=AX.X),
            waits=[(sc, s_i)], inc=vx)
        p.op("vector", lambda e, i=i: e.reduce_sum(
            sm[:, i:i + 1], xbuf[:, i * D:(i + 1) * D], axis=AX.X),
            waits=[(ld, 16 * (i + 1))], inc=vx)
    w1_ = p.op("vector", lambda e: e.tensor_scalar(sm[:], sm[:], 1.0 / D, None, ALU.mult),
               waits=[(vx, vx.n), (sc, sc.n)], inc=vx)
    w2_ = p.op("vector", lambda e: e.tensor_scalar(sq[:], sq[:], 1.0 / D, None, ALU.mult),
               waits=[(vx, w1_)], inc=vx)
    w3_ = p.op("vector", lambda e: e.tensor_tensor(tmp[:], sm[:], sm[:], ALU.mult),
               waits=[(vx, w2_)], inc=vx)
    w4_ = p.op("vector", lambda e: e.tensor_sub(sq[:], sq[:], tmp[:]),
               waits=[(vx, w3_)], inc=vx)
    w5_ = p.op("scalar", lambda e: e.activation(rstd[:], sq[:], AF.Sqrt,
                                                bias=ctx.constf(1e-5)),
               waits=[(vx, vx.n)], inc=sc)
    w6_ = p.op("vector", lambda e: e.reciprocal(rstd[:], rstd[:]),
               waits=[(sc, sc.n)], inc=vx)
    p.op("vector", lambda e: e.scalar_tensor_tensor(
        negmr[:], sm[:], -1.0, rstd[:], ALU.mult, ALU.mult),
        waits=[(vx, w6_)], inc=vx)
    p.op("vector", lambda e: e.memset(mx[:], 0.0), inc=vx)
    stats_done = vx.n

    for i in range(NTT):
        p.op("scalar", lambda e, i=i: e.activation(
            xnt[:], xbuf[:, i * D:(i + 1) * D], AF.Identity,
            bias=negmr[:, i:i + 1], scale=rstd[:, i:i + 1]),
            waits=[(vx, stats_done if i == 0 else vx.n), (tp, tp.n)], inc=sc)
        sc_i = sc.n
        vcp = p.op("vector", lambda e: e.tensor_copy(xnt2[:], xnt[:]),
                   waits=[(sc, sc_i)], inc=vx)
        wm_ = p.op("vector", lambda e: e.reduce_max(mxt[:], xnt2[:], axis=AX.X,
                                                    apply_absolute_value=True),
                   waits=[(vx, vcp)], inc=vx)
        p.op("vector", lambda e: e.tensor_max(mx[:], mx[:], mxt[:]),
             waits=[(vx, vx.n)], inc=vx)
        for c in range(NCH):
            tp_i = p.op("tensor", lambda e, c=c: e.transpose(
                pst[:], xnt2[:, c * 128:(c + 1) * 128], ctx.c["ident_f"][:]),
                waits=[(vx, vx.n)], inc=tp)
            p.op("vector", lambda e, c=c, i=i: e.tensor_copy(
                xa[:, c * BT + i * 128: c * BT + (i + 1) * 128], pst[:]),
                waits=[(tp, tp_i)], inc=vx)
    evall = vx.n
    for c in range(NCH):
        p.op("sync", lambda e, c=c: e.dma_start(
            out=xnFM[c], in_=xa[:, c * BT:(c + 1) * BT]), waits=[(vx, evall)],
            inc=st, k=16)
    lg0 = p.sem()
    g1_ = p.op("gpsimd", lambda e: e.dma_start(out=mxD, in_=mx[:]),
               waits=[(vx, evall)], inc=lg0, k=16)
    g2_ = p.op("gpsimd", lambda e: e.dma_start(
        out=m1row[:], in_=mxD.rearrange("p o -> o p")), waits=[(lg0, g1_)],
        inc=lg0, k=16)
    p.op("vector", lambda e: e.reduce_max(m1o[:], m1row[:], axis=AX.X),
         waits=[(lg0, g2_)], inc=vx)
    p.op("sync", lambda e: e.dma_start(out=m1_part, in_=m1o[:]),
         waits=[(vx, vx.n)], inc=st, k=16)
    ctx.c["xa"] = xa
    ctx.c["ln0"] = dict(rstd=rstd, negmr=negmr)
    p.drain(st)
    S.close()
    p.barrier()


def emit_beta(ctx, wb_in, crl_in):
    p = ctx.p
    xa = ctx.c["xa"]
    crlrep = p.sb("crlrep", [128, H], F32)
    ep = getattr(ctx, "epoch", None)
    betaFM = p.sb("betaFM", [4, BT], F32, scope=ep)
    betaTM = p.sb("betaTM", [128, NTT * H], F32, scope=ep)
    S = ExitStack()
    wb = p.sb("wb_s", [128, NCH * H], BF16, scope=S)
    crl = p.sb("crl_s", [1, H], F32, scope=S)
    crlb = p.sb("crlb", [1, H], BF16, scope=S)
    psb = p.ps("ps_beta", [4, 512], F32, scope=S)
    pst = p.ps("ps_bt", [128, 128], F32, scope=S)
    ld = p.sem()
    mm = p.sem()
    ev = p.sem()

    p.op("gpsimd", lambda e: e.dma_start(
        out=wb[:], in_=AP(wb_in.tensor, 0, [[4, 128], [512, NCH], [1, H]])),
        inc=ld, k=16)
    p.op("gpsimd", lambda e: e.dma_start(
        out=crl[:], in_=crl_in.rearrange("(o h) -> o h", o=1)), inc=ld, k=16)
    p.op("scalar", lambda e: e.activation(crl[:], crl[:], AF.Sigmoid),
         waits=[(ld, 32)], inc=ev)
    p.op("vector", lambda e: e.tensor_copy(crlb[:], crl[:]), waits=[(ev, ev.n)], inc=ev)
    p.op("tensor", lambda e: e.matmul(pst[:, 0:H], ctx.c["ones_col"][:], crlb[:],
                                      start=True, stop=True),
         waits=[(ev, ev.n)], inc=mm)
    p.op("vector", lambda e: e.tensor_copy(crlrep[:], pst[:, 0:H]),
         waits=[(mm, mm.n)], inc=ev)

    for n in range(BT // 512):
        for c in range(NCH):
            p.op("tensor", lambda e, c=c, n=n: e.matmul(
                psb[:], wb[:, c * H:(c + 1) * H],
                xa[:, c * BT + n * 512: c * BT + (n + 1) * 512],
                start=(c == 0), stop=(c == NCH - 1)),
                waits=([(ld, 16), (ev, ev.n)] if c == 0 else []), inc=mm)
        p.op("scalar", lambda e, n=n: e.activation(
            betaFM[:, n * 512:(n + 1) * 512], psb[:], AF.Sigmoid),
            waits=[(mm, mm.n)], inc=ev)
    for i in range(NTT):
        t_i = p.op("tensor", lambda e, i=i: e.transpose(
            pst[:, 0:H], betaFM[:, i * 128:(i + 1) * 128], ctx.c["ident_f"][0:H, 0:H]),
            waits=[(ev, ev.n)], inc=mm)
        p.op("vector", lambda e, i=i: e.tensor_copy(
            betaTM[:, i * H:(i + 1) * H], pst[:, 0:H]), waits=[(mm, t_i)], inc=ev)
    ctx.c["beta"] = dict(FM=betaFM, TM=betaTM, crlrep=crlrep)
    S.close()
    p.barrier()


PAD = 64


def emit_qkv(ctx, w_ins, cw_ins, outFM, want_norm=(True, True, False), dbg=None):
    p = ctx.p
    xa = ctx.c["xa"]
    names = ("q", "k", "v")
    ep = getattr(ctx, "epoch", None)
    invn = {}
    for ti, nm in enumerate(names):
        if want_norm[ti]:
            invn[nm] = dict(
                FM=p.sb(f"invn_{nm}_FM", [4, BT], F32, scope=ep),
                TM=p.sb(f"invn_{nm}_TM", [128, NTT * H], F32, scope=ep))
    S = ExitStack()
    wbuf = p.sb("qkv_w", [128, NCH * D], BF16, scope=S)
    cw = p.sb("qkv_cw", [128, NCH * 4], F32, scope=S)
    diagb = p.sb("qkv_diag", [128, NCH * 4 * 128], BF16, scope=S)
    pbuf = p.sb("qkv_pbuf", [128, 4224], BF16, scope=S)
    cbuf = p.sb("qkv_cbuf", [128, BT], BF16, scope=S)
    sqt = p.sb("qkv_sqt", [128, 512], BF16, scope=S)
    ssq = p.sb("qkv_ssq", [4, BT], F32, scope=S)
    psA = p.ps("qkv_psA", [128, 512], F32, scope=S)
    psC = p.ps("qkv_psC", [128, 512], F32, scope=S)
    psS = p.ps("qkv_psS", [4, 512], F32, scope=S)
    pst = p.ps("qkv_pst", [128, 128], F32, scope=S)

    for ti, nm in enumerate(names):
        ld = p.sem()
        mm = p.sem()
        mm2 = p.sem()
        vx = p.sem()
        sc = p.sem()
        st = p.sem()
        w_in, cw_in = w_ins[ti], cw_ins[ti]
        p.op("gpsimd", lambda e, w_in=w_in: e.dma_start(
            out=wbuf[:], in_=AP(w_in.tensor, 0, [[1024, 128], [131072, NCH], [1, D]])),
            inc=ld, k=16)
        p.op("gpsimd", lambda e, cw_in=cw_in: e.dma_start(
            out=cw[:], in_=AP(cw_in.tensor, 0, [[4, 128], [512, NCH], [1, 4]])),
            inc=ld, k=16)
        p.op("gpsimd", lambda e: e.memset(pbuf[:], 0.0), inc=vx)
        p.op("vector", lambda e: e.memset(ssq[:], 0.0), inc=vx)
        for co in range(NCH):
            for j in range(4):
                p.op("vector", lambda e, co=co, j=j: e.tensor_scalar(
                    diagb[:, (co * 4 + j) * 128:(co * 4 + j + 1) * 128],
                    ctx.c["ident_bf"][:], cw[:, co * 4 + (3 - j):co * 4 + (3 - j) + 1],
                    None, ALU.mult), waits=[(ld, 32)] if co == 0 and j == 0 else [],
                    inc=vx)
        setup_done = vx.n
        for co in range(NCH):
            h = co // 2
            for n in range(NCH):
                b, nn = n // 4, n % 4
                for ci in range(NCH):
                    p.op("tensor", lambda e, co=co, n=n, ci=ci: e.matmul(
                        psA[:], wbuf[:, ci * D + co * 128: ci * D + (co + 1) * 128],
                        xa[:, ci * BT + n * 512: ci * BT + (n + 1) * 512],
                        start=(ci == 0), stop=(ci == NCH - 1)),
                        waits=([(ld, 16), (vx, vx.n), (mm2, mm2.n)] if ci == 0 else []),
                        inc=mm)
                p.op("vector", lambda e, b=b, nn=nn: e.tensor_copy(
                    pbuf[:, b * 2112 + PAD + nn * 512:
                         b * 2112 + PAD + (nn + 1) * 512], psA[:]),
                    waits=[(mm, mm.n)] + ([(mm2, mm2.n)] if n == 0 else []), inc=vx)
            for b in range(B):
                for n in range(4):
                    base = b * 2112 + PAD + n * 512
                    for j in range(4):
                        p.op("tensor", lambda e, co=co, j=j, base=base: e.matmul(
                            psC[:], diagb[:, (co * 4 + j) * 128:(co * 4 + j + 1) * 128],
                            pbuf[:, base - j: base - j + 512],
                            start=(j == 0), stop=(j == 3)),
                            waits=([(vx, vx.n), (sc, sc.n)] if j == 0 else []), inc=mm2)
                    p.op("scalar", lambda e, b=b, n=n: e.activation(
                        cbuf[:, b * 2048 + n * 512: b * 2048 + (n + 1) * 512],
                        psC[:], AF.Silu),
                        waits=[(mm2, mm2.n)] + ([(st, st.n)] if (b == 0 and n == 0) else []),
                        inc=sc)
            p.op("sync", lambda e, co=co, dst=outFM[ti]: e.dma_start(
                out=dst[co], in_=cbuf[:]), waits=[(sc, sc.n)], inc=st, k=16)
            if want_norm[ti]:
                for n2 in range(NCH):
                    p.op("scalar", lambda e, n2=n2: e.activation(
                        sqt[:], cbuf[:, n2 * 512:(n2 + 1) * 512], AF.Square),
                        waits=[(mm, mm.n)], inc=sc)
                    p.op("tensor", lambda e, co=co, n2=n2: e.matmul(
                        psS[:], ctx.c[f"hmask{co}"][:], sqt[:],
                        start=True, stop=True), waits=[(sc, sc.n), (vx, vx.n)], inc=mm)
                    p.op("vector", lambda e, n2=n2: e.tensor_tensor(
                        ssq[:, n2 * 512:(n2 + 1) * 512],
                        ssq[:, n2 * 512:(n2 + 1) * 512], psS[:], ALU.add),
                        waits=[(mm, mm.n)], inc=vx)
        if dbg is not None and ti == 0:
            sdb = p.sem()
            p.op("sync", lambda e: e.dma_start(out=dbg[0], in_=pbuf[:]), inc=sdb, k=16)
            p.op("sync", lambda e: e.dma_start(out=dbg[1][:, 0:BT], in_=cbuf[:]), inc=sdb, k=16)
            p.op("sync", lambda e: e.dma_start(out=dbg[2][:, 0:4 * 128], in_=diagb[:, 7 * 4 * 128:8 * 4 * 128]), inc=sdb, k=16)
            p.op("sync", lambda e: e.nop(), waits=[(sdb, 48)])
        if want_norm[ti]:
            nm_ = nm
            iv = invn[nm_]
            p.op("scalar", lambda e, iv=iv: e.activation(
                iv["FM"][:], ssq[:], AF.Sqrt, bias=ctx.constf(1e-12, 4)),
                waits=[(vx, vx.n)], inc=sc)
            p.op("vector", lambda e, iv=iv: e.reciprocal(iv["FM"][:], iv["FM"][:]),
                 waits=[(sc, sc.n)], inc=vx)
            for i in range(NTT):
                t_i = p.op("tensor", lambda e, iv=iv, i=i: e.transpose(
                    pst[:, 0:H], iv["FM"][:, i * 128:(i + 1) * 128],
                    ctx.c["ident_f"][0:H, 0:H]),
                    waits=[(vx, vx.n)], inc=mm)
                p.op("vector", lambda e, iv=iv, i=i: e.tensor_copy(
                    iv["TM"][:, i * H:(i + 1) * H], pst[:, 0:H]),
                    waits=[(mm, t_i)], inc=vx)
        p.drain(st)
        p.barrier()
    for nm in invn:
        ctx.c[f"invn_{nm}"] = invn[nm]
    S.close()


def emit_hmasks(ctx):
    p = ctx.p
    for co in range(NCH):
        m = p.sb(f"hmask{co}", [128, 4], BF16)
        h = co // 2
        p.op("gpsimd", lambda e, m=m: e.memset(m[:], 0.0))
        p.op("gpsimd", lambda e, m=m, h=h: e.memset(m[:, h:h + 1], 1.0))
        ctx.c[f"hmask{co}"] = m
    p.barrier()


def emit_delta(ctx, qFM, kFM, vFM, doTM, dbg=None):
    p = ctx.p
    S = ExitStack()
    qh = p.sb("dl_q", [128, 2 * L], BF16, scope=S)
    kh = p.sb("dl_k", [128, 2 * L], BF16, scope=S)
    kb = p.sb("dl_kb", [128, 2 * L], BF16, scope=S)
    vb = p.sb("dl_vb", [128, 2 * L], BF16, scope=S)
    ktm = p.sb("dl_ktm", [128, NT * DK], BF16, scope=S)
    kbtm = p.sb("dl_kbtm", [128, NT * DK], BF16, scope=S)
    vbtm = p.sb("dl_vbtm", [128, NT * DV], BF16, scope=S)
    invqb = p.sb("dl_invqb", [1, L], BF16, scope=S)
    invkb = p.sb("dl_invkb", [1, L], BF16, scope=S)
    betab = p.sb("dl_betab", [1, L], BF16, scope=S)
    Bm = p.sb("dl_B", [128, 128], BF16, scope=S)
    Bt = p.sb("dl_Bt", [128, 128], BF16, scope=S)
    M2 = p.sb("dl_M2", [128, 128], BF16, scope=S)
    M2t = p.sb("dl_M2t", [128, 128], BF16, scope=S)
    Pm = p.sb("dl_P", [128, 128], BF16, scope=S)
    Pt = p.sb("dl_Pt", [128, 128], BF16, scope=S)
    atT = p.sb("dl_atT", [128, 128], BF16, scope=S)
    u_sb = p.sb("dl_u", [128, DV], BF16, scope=S)
    ui_sb = p.sb("dl_ui", [128, DV], BF16, scope=S)
    wfm = p.sb("dl_wfm", [128, 2 * 128], BF16, scope=S)
    o_sb = p.sb("dl_o", [128, DV], BF16, scope=S)
    S_sb = p.sb("dl_S", [128, 2 * DV], BF16, scope=S)
    pML0 = p.ps("dl_pML0", [128, 128], F32, scope=S)
    pML1 = p.ps("dl_pML1", [128, 128], F32, scope=S)
    pU = p.ps("dl_pU", [128, DV], F32, scope=S)
    pW = p.ps("dl_pW", [128, DV], F32, scope=S)
    psS0 = p.ps("dl_psS0", [128, DV], F32, scope=S)
    psS1 = p.ps("dl_psS1", [128, DV], F32, scope=S)
    pB128 = p.ps("dl_pB128", [128, 512], F32, scope=S)

    ld = p.sem()
    mm = p.sem()
    vx = p.sem()
    sc = p.sem()
    gp = p.sem()
    st = p.sem()

    iq = ctx.c["invn_q"]["FM"]
    ik = ctx.c["invn_k"]["FM"]
    bFM = ctx.c["beta"]["FM"]
    ones = ctx.c["ones_col"]
    msl = ctx.c["mask_sl"]
    msu = ctx.c["mask_su"]
    mui = ctx.c["mask_ui"]
    idb = ctx.c["ident_bf"]

    p.barrier()

    def T(fn, waits=()):
        return p.op("tensor", fn, waits=waits, inc=mm)

    def V(fn, waits=()):
        return p.op("vector", fn, waits=waits, inc=vx)

    def C(fn, waits=()):
        return p.op("scalar", fn, waits=waits, inc=sc)

    def G(fn, waits=()):
        return p.op("gpsimd", fn, waits=waits, inc=gp)

    for b in range(B):
        for h in range(H):
            l0 = ld.n
            for (dst, srcr) in ((invqb, iq), (invkb, ik), (betab, bFM)):
                p.op("gpsimd", lambda e, dst=dst, srcr=srcr, b=b, h=h: e.dma_start(
                    out=dst[0:1, :], in_=srcr[h:h + 1, b * L:(b + 1) * L]),
                    waits=[(mm, mm.n)], inc=ld, k=16)
            for c2 in range(2):
                for (buf, src) in ((qh, qFM), (kh, kFM), (vb, vFM)):
                    p.op("sync", lambda e, c2=c2, buf=buf, src=src, b=b, h=h:
                         e.dma_start(out=buf[:, c2 * L:(c2 + 1) * L],
                                     in_=src[2 * h + c2][:, b * L:(b + 1) * L]),
                         waits=[(st, st.n), (vx, vx.n), (mm, mm.n)], inc=ld, k=16)
            lq = ld.n
            for c2 in range(2):
                for n in range(4):
                    cols = slice(c2 * L + n * 512, c2 * L + (n + 1) * 512)
                    rb0, rb1 = n * 512, (n + 1) * 512
                    t1 = T(lambda e, h=h, rb0=rb0, rb1=rb1: e.matmul(
                        pB128[:], ones[:], invqb[0:1, rb0:rb1], start=True, stop=True),
                        waits=[(ld, lq), (vx, vx.n)])
                    V(lambda e, cols=cols: e.tensor_tensor(
                        qh[:, cols], qh[:, cols], pB128[:], ALU.mult),
                        waits=[(mm, t1)])
                    t2 = T(lambda e, h=h, rb0=rb0, rb1=rb1: e.matmul(
                        pB128[:], ones[:], invkb[0:1, rb0:rb1], start=True, stop=True),
                        waits=[(vx, vx.n)])
                    V(lambda e, cols=cols: e.tensor_tensor(
                        kh[:, cols], kh[:, cols], pB128[:], ALU.mult),
                        waits=[(mm, t2)])
                    t3 = T(lambda e, h=h, rb0=rb0, rb1=rb1: e.matmul(
                        pB128[:], ones[:], betab[0:1, rb0:rb1], start=True, stop=True),
                        waits=[(vx, vx.n)])
                    V(lambda e, cols=cols: e.tensor_tensor(
                        kb[:, cols], kh[:, cols], pB128[:], ALU.mult),
                        waits=[(mm, t3)])
                    V(lambda e, cols=cols: e.tensor_tensor(
                        vb[:, cols], vb[:, cols], pB128[:], ALU.mult))
            sc_v = vx.n
            for jt in range(NT):
                for c2 in range(2):
                    for (srcb, dstb) in ((kh, ktm), (kb, kbtm), (vb, vbtm)):
                        tt = T(lambda e, jt=jt, c2=c2, srcb=srcb: e.transpose(
                            pML0.bitcast(BF16)[:, 0:128],
                            srcb[:, c2 * L + jt * 128: c2 * L + (jt + 1) * 128],
                            idb[:]),
                            waits=[(vx, vx.n)])
                        V(lambda e, jt=jt, c2=c2, dstb=dstb: e.tensor_copy(
                            dstb[:, jt * 256 + c2 * 128: jt * 256 + (c2 + 1) * 128],
                            pML0.bitcast(BF16)[:, 0:128]),
                            waits=[(mm, tt)])
            V(lambda e: e.memset(S_sb[:], 0.0))
            setup_v = vx.n

            for i in range(NT):
                csl = [slice(c2 * L + i * 128, c2 * L + (i + 1) * 128) for c2 in range(2)]
                for c2 in range(2):
                    T(lambda e, c2=c2, csl=csl: e.matmul(
                        pML0[:], kb[:, csl[c2]], kh[:, csl[c2]],
                        start=(c2 == 0), stop=(c2 == 1)),
                        waits=([(vx, setup_v if i == 0 else vx.n)] if c2 == 0 else []))
                    T(lambda e, c2=c2, csl=csl: e.matmul(
                        pML1[:], kh[:, csl[c2]], kb[:, csl[c2]],
                        start=(c2 == 0), stop=(c2 == 1)))
                m_a = mm.n
                vB = V(lambda e: e.scalar_tensor_tensor(
                    Bm[:], pML0[:], -1.0, msl[:], ALU.mult, ALU.mult),
                    waits=[(mm, m_a)])
                sB = V(lambda e: e.scalar_tensor_tensor(
                    Bt[:], pML1[:], -1.0, msu[:], ALU.mult, ALU.mult),
                    waits=[(mm, m_a)])
                vP = G(lambda e: e.tensor_add(Pm[:], Bm[:], idb[:]), waits=[(vx, vB)])
                sP = G(lambda e: e.tensor_add(Pt[:], Bt[:], idb[:]), waits=[(vx, sB)])
                curq, curqt = Bm, Bt
                for s_ in range(1, 7):
                    t1 = T(lambda e, curq=curq, curqt=curqt: e.matmul(
                        pML0[:], curqt[:], curq[:], start=True, stop=True),
                        waits=[(vx, vx.n), (sc, sc.n), (gp, gp.n)])
                    t2 = T(lambda e, curq=curq, curqt=curqt: e.matmul(
                        pML1[:], curq[:], curqt[:], start=True, stop=True))
                    vE = V(lambda e: e.tensor_copy(M2[:], pML0[:]), waits=[(mm, t1)])
                    sE = C(lambda e: e.copy(M2t[:], pML1[:]), waits=[(mm, t2)])
                    t3 = T(lambda e: e.matmul(pML0[:], Pt[:], M2[:], start=True, stop=False),
                           waits=[(vx, vE), (sc, sE)])
                    T(lambda e: e.matmul(pML0[:], idb[:], Pm[:], start=False, stop=True))
                    t4 = T(lambda e: e.matmul(pML1[:], M2[:], Pt[:], start=True, stop=False))
                    t5 = T(lambda e: e.matmul(pML1[:], idb[:], Pt[:], start=False, stop=True))
                    vP2 = V(lambda e: e.tensor_copy(Pm[:], pML0[:]), waits=[(mm, t5)])
                    sP2 = C(lambda e: e.copy(Pt[:], pML1[:]), waits=[(mm, t5)])
                    curq, curqt = M2, M2t
                tU = T(lambda e, i=i: e.matmul(
                    pU[:], Pt[:], vbtm[:, i * 256:(i + 1) * 256], start=True, stop=True),
                    waits=[(vx, vx.n), (sc, sc.n), (gp, gp.n)])
                vU = V(lambda e: e.tensor_copy(u_sb[:], pU[:]), waits=[(mm, tU)])
                for c2 in range(2):
                    tW = T(lambda e, i=i, c2=c2: e.matmul(
                        pML0[:], kbtm[:, i * 256 + c2 * 128: i * 256 + (c2 + 1) * 128],
                        Pt[:], start=True, stop=True),
                        waits=[(vx, vx.n)])
                    V(lambda e, c2=c2: e.tensor_copy(
                        wfm[:, c2 * 128:(c2 + 1) * 128], pML0[:]), waits=[(mm, tW)])
                vW = vx.n
                for c2 in range(2):
                    T(lambda e, c2=c2: e.matmul(
                        pW[:], wfm[:, c2 * 128:(c2 + 1) * 128],
                        S_sb[:, c2 * DV:(c2 + 1) * DV],
                        start=(c2 == 0), stop=(c2 == 1)),
                        waits=[(vx, vW), (sc, sc.n)] if c2 == 0 else [])
                mWS = mm.n
                vUI = V(lambda e: e.tensor_sub(ui_sb[:], u_sb[:], pW[:]),
                        waits=[(mm, mWS), (vx, vU)])
                for c2 in range(2):
                    T(lambda e, c2=c2, csl=csl: e.matmul(
                        pML1[:], kh[:, csl[c2]], qh[:, csl[c2]],
                        start=(c2 == 0), stop=(c2 == 1)),
                        waits=[(sc, sc.n)] if c2 == 0 else [])
                mAT = mm.n
                vAT = V(lambda e: e.tensor_tensor(atT[:], pML1[:], mui[:], ALU.mult),
                        waits=[(mm, mAT)])
                for c2 in range(2):
                    T(lambda e, c2=c2, csl=csl: e.matmul(
                        pU[:], qh[:, csl[c2]], S_sb[:, c2 * DV:(c2 + 1) * DV],
                        start=(c2 == 0), stop=False),
                        waits=[(vx, vUI), (sc, sc.n)] if c2 == 0 else [])
                tO = T(lambda e: e.matmul(pU[:], atT[:], ui_sb[:],
                                          start=False, stop=True),
                       waits=[(vx, vAT)])
                vO = V(lambda e: e.tensor_copy(o_sb[:], pU[:]), waits=[(mm, tO)])
                p.op("sync", lambda e, b=b, h=h, i=i: e.dma_start(
                    out=doTM[b][h][i], in_=o_sb[:]), waits=[(vx, vO)], inc=st, k=16)
                if dbg is not None and b == 0 and h == 0 and i == 0:
                    sdb2 = p.sem()
                    p.op("sync", lambda e: e.dma_start(out=dbg[0][:, 0:DV], in_=ui_sb[:]),
                         waits=[(vx, vx.n), (sc, sc.n)], inc=sdb2, k=16)
                    p.op("sync", lambda e: e.dma_start(out=dbg[1][:, 0:256], in_=wfm[:]),
                         waits=[(vx, vx.n)], inc=sdb2, k=16)
                    p.op("sync", lambda e: e.dma_start(out=dbg[2][:, 0:DV], in_=u_sb[:]),
                         inc=sdb2, k=16)
                for c2 in range(2):
                    T(lambda e, i=i, c2=c2, ps=(psS0 if c2 == 0 else psS1):
                      e.matmul(ps[:],
                               ktm[:, i * 256 + c2 * 128: i * 256 + (c2 + 1) * 128],
                               ui_sb[:], start=(i == 0), stop=(i == NT - 1)),
                      waits=[(vx, vx.n), (sc, sc.n)] if c2 == 0 else [])
                mSU = mm.n
                V(lambda e: e.tensor_copy(S_sb[:, 0:DV], psS0[:]), waits=[(mm, mSU)])
                C(lambda e: e.copy(S_sb[:, DV:2 * DV], psS1[:]), waits=[(mm, mSU)])
                if dbg is not None and b == 0 and h == 0 and i == 0:
                    sdb3 = p.sem()
                    p.op("sync", lambda e: e.dma_start(out=dbg[3][:, 0:2 * DV], in_=S_sb[:]),
                         waits=[(vx, vx.n), (sc, sc.n)], inc=sdb3, k=16)
                    p.op("sync", lambda e: e.dma_start(out=dbg[4][:, 0:DV * 2],
                                                       in_=ktm[:, 0:512]),
                         inc=sdb3, k=16)
    p.drain(st)
    S.close()
    p.barrier()


def emit_dwconv_diag(ctx, inFM, w_in, K, outFM, act=None):
    p = ctx.p
    S = ExitStack()
    cw = p.sb("dwc_cw", [128, NCH * K], F32, scope=S)
    diagb = p.sb("dwc_diag", [128, NCH * K * 128], BF16, scope=S)
    pbuf = p.sb("dwc_pbuf", [128, 4224], BF16, scope=S)
    cbuf = p.sb("dwc_cbuf", [128, BT], BF16, scope=S)
    psC = p.ps("dwc_psC", [128, 512], F32, scope=S)
    ld = p.sem()
    mm = p.sem()
    vx = p.sem()
    sc = p.sem()
    st = p.sem()

    p.op("gpsimd", lambda e: e.dma_start(
        out=cw[:], in_=AP(w_in.tensor, 0, [[K, 128], [128 * K, NCH], [1, K]])),
        inc=ld, k=16)
    p.op("gpsimd", lambda e: e.memset(pbuf[:], 0.0), inc=vx)
    for co in range(NCH):
        for j in range(K):
            p.op("vector", lambda e, co=co, j=j: e.tensor_scalar(
                diagb[:, (co * K + j) * 128:(co * K + j + 1) * 128],
                ctx.c["ident_bf"][:], cw[:, co * K + j:co * K + j + 1],
                None, ALU.mult), waits=[(ld, 16)] if co == 0 and j == 0 else [],
                inc=vx)
    for co in range(NCH):
        for b in range(B):
            p.op("sync", lambda e, co=co, b=b: e.dma_start(
                out=pbuf[:, b * 2112 + PAD: b * 2112 + PAD + L],
                in_=inFM[co][:, b * L:(b + 1) * L]),
                waits=[(vx, vx.n), (mm, mm.n)], inc=ld, k=16)
        ldc = ld.n
        for b in range(B):
            for n in range(4):
                base = b * 2112 + PAD + n * 512
                for j in range(K):
                    p.op("tensor", lambda e, co=co, j=j, base=base: e.matmul(
                        psC[:], diagb[:, (co * K + j) * 128:(co * K + j + 1) * 128],
                        pbuf[:, base - j: base - j + 512],
                        start=(j == 0), stop=(j == K - 1)),
                        waits=([(ld, ldc), (sc, sc.n)] if j == 0 else []), inc=mm)
                if act is None:
                    p.op("scalar", lambda e, b=b, n=n: e.copy(
                        cbuf[:, b * 2048 + n * 512: b * 2048 + (n + 1) * 512], psC[:]),
                        waits=[(mm, mm.n)] + ([(st, st.n)] if (b == 0 and n == 0) else []),
                        inc=sc)
                else:
                    p.op("scalar", lambda e, b=b, n=n: e.activation(
                        cbuf[:, b * 2048 + n * 512: b * 2048 + (n + 1) * 512],
                        psC[:], act),
                        waits=[(mm, mm.n)] + ([(st, st.n)] if (b == 0 and n == 0) else []),
                        inc=sc)
        p.op("sync", lambda e, co=co: e.dma_start(
            out=outFM[co], in_=cbuf[:]), waits=[(sc, sc.n)], inc=st, k=16)
    p.drain(st)
    S.close()
    p.barrier()


def emit_do_fm(ctx, doTM, doFM):
    p = ctx.p
    S = ExitStack()
    tbuf = p.sb("dof_t", [128, 256], BF16, scope=S)
    obuf = p.sb("dof_o", [128, BT], BF16, scope=S)
    pst = p.ps("dof_ps", [128, 128], BF16, scope=S)
    ld = p.sem()
    mm = p.sem()
    vx = p.sem()
    st = p.sem()
    for c in range(NCH):
        h, c2 = c // 2, c % 2
        for b in range(B):
            for jt in range(NT):
                l1 = p.op("sync", lambda e, b=b, h=h, jt=jt: e.dma_start(
                    out=tbuf[:], in_=doTM[b][h][jt]),
                    waits=[(mm, mm.n)], inc=ld, k=16)
                t1 = p.op("tensor", lambda e, c2=c2: e.transpose(
                    pst[:], tbuf[:, c2 * 128:(c2 + 1) * 128], ctx.c["ident_bf"][:]),
                    waits=[(ld, l1), (vx, vx.n)], inc=mm)
                p.op("vector", lambda e, b=b, jt=jt: e.tensor_copy(
                    obuf[:, b * L + jt * 128: b * L + (jt + 1) * 128], pst[:]),
                    waits=[(mm, t1)] + ([(st, st.n)] if (b == 0 and jt == 0) else []),
                    inc=vx)
        p.op("sync", lambda e, c=c: e.dma_start(out=doFM[c], in_=obuf[:]),
             waits=[(vx, vx.n)], inc=st, k=16)
    p.drain(st)
    S.close()
    p.barrier()


def emit_fir_long(ctx, vFM, w_in, outFM, TB, wpad):
    p = ctx.p
    S = ExitStack()
    vtm = p.sb("fl_vtm", [128, 17 * 1024], BF16, scope=S)
    vch = p.sb("fl_vch", [128, L], BF16, scope=S)
    wrow = p.sb("fl_wrow", [128, 320], BF16, scope=S)
    tc8 = [p.sb(f"fl_tc8_{i}", [128, 8 * 128], BF16, scope=S) for i in range(2)]
    th8 = [p.sb(f"fl_th8_{i}", [128, 8 * 128], BF16, scope=S) for i in range(2)]
    chb = p.sb("fl_chb", [16, 128 * 128], BF16, scope=S)
    pst = p.ps("fl_pst", [128, 128], BF16, scope=S)
    psC = p.ps("fl_psC", [16, 128], F32, scope=S)
    ld = p.sem()
    mm = p.sem()
    vx = p.sem()
    sc = p.sem()
    st = p.sem()
    tb = p.sem()

    for c in range(NCH):
        m1 = p.op("gpsimd", lambda e: e.memset(wrow[:], 0.0), waits=[(ld, ld.n)], inc=vx)
        p.op("gpsimd", lambda e, c=c: e.dma_start(
            out=wrow[:, 127:191],
            in_=AP(w_in.tensor, c * 128 * 64, [[64, 128], [1, 64]])),
            waits=[(vx, m1)], inc=ld, k=16)
        p.op("gpsimd", lambda e, c=c: e.dma_start(
            out=AP(wpad.tensor, c * 128 * 320, [[320, 128], [1, 320]]), in_=wrow[:]),
            waits=[(ld, ld.n)], inc=ld, k=16)
    wp_done = ld.n
    for r in range(192):
        s_ = -r if r < 128 else 192 - r
        p.op("gpsimd", lambda e, r=r, s_=s_: e.dma_start(
            out=AP(TB.tensor, r * 128, [[192 * 128, 1024], [1, 128]]),
            in_=AP(wpad.tensor, 127 + s_, [[320, 1024], [1, 128]])),
            waits=[(ld, wp_done)], inc=tb, k=16)
    tb_done = tb.n

    for b in range(B):
        p.op("vector", lambda e: e.memset(vtm[:, 0:1024], 0.0),
             waits=[(mm, mm.n), (sc, sc.n)], inc=vx)
        for c in range(NCH):
            lv = p.op("sync", lambda e, c=c, b=b: e.dma_start(
                out=vch[:], in_=vFM[c][:, b * L:(b + 1) * L]),
                waits=[(vx, vx.n), (mm, mm.n)], inc=ld, k=16)
            for jt in range(NT):
                t1 = p.op("tensor", lambda e, jt=jt: e.transpose(
                    pst[:], vch[:, jt * 128:(jt + 1) * 128],
                    ctx.c["ident_bf"][:]), waits=[(ld, lv), (vx, vx.n)], inc=mm)
                p.op("vector", lambda e, jt=jt, c=c: e.tensor_copy(
                    vtm[:, (1 + jt) * 1024 + c * 128: (1 + jt) * 1024 + (c + 1) * 128],
                    pst[:]), waits=[(mm, t1)], inc=vx)
        vtm_done = vx.n
        vr = vtm[:].rearrange("p (a d) -> p a d", d=1024)
        for g in range(128):
            gi = g % 2
            lg = p.op("sync", lambda e, g=g, gi=gi: e.dma_start(
                out=tc8[gi][:], in_=AP(TB.tensor, g * 8 * 192 * 128,
                                       [[128, 128], [192 * 128, 8], [1, 128]])),
                waits=[(tb, tb_done), (mm, mm.n)], inc=ld, k=16)
            lg2 = p.op("sync", lambda e, g=g, gi=gi: e.dma_start(
                out=th8[gi][64:128, :],
                in_=AP(TB.tensor, g * 8 * 192 * 128 + 128 * 128,
                       [[128, 64], [192 * 128, 8], [1, 128]])),
                inc=ld, k=16)
            for cc in range(8):
                c = g * 8 + cc
                col = c % 128
                cur_ap = vr[:, 1:17, c]
                hist_ap = vr[64:128, 0:16, c]
                t1 = p.op("tensor", lambda e, cur_ap=cur_ap, gi=gi, cc=cc: e.matmul(
                    psC[:], cur_ap, tc8[gi][:, cc * 128:(cc + 1) * 128],
                    start=True, stop=False),
                    waits=[(ld, lg2), (vx, vx.n), (sc, sc.n)], inc=mm)
                t2 = p.op("tensor", lambda e, hist_ap=hist_ap, gi=gi, cc=cc: e.matmul(
                    psC[:], hist_ap, th8[gi][64:128, cc * 128:(cc + 1) * 128],
                    start=False, stop=True), inc=mm)
                eng = "vector" if (cc % 2 == 0) else "scalar"
                sem = vx if (cc % 2 == 0) else sc
                ewaits = [(mm, t2)] + ([(st, st.n)] if col == 0 else [])
                if eng == "vector":
                    p.op(eng, lambda e, col=col: e.tensor_copy(
                        chb[:, col * 128:(col + 1) * 128], psC[:]),
                        waits=ewaits, inc=sem)
                else:
                    p.op(eng, lambda e, col=col: e.copy(
                        chb[:, col * 128:(col + 1) * 128], psC[:]),
                        waits=ewaits, inc=sem)
                if col == 127:
                    p.op("sync", lambda e, c=c, b=b: e.dma_start(
                        out=outFM[c // 128][:, b * L:(b + 1) * L].rearrange(
                            "p (a d) -> a p d", d=128),
                        in_=chb[:].rearrange("p (cc2 d) -> p cc2 d", d=128)),
                        waits=[(vx, vx.n), (sc, sc.n)], inc=st, k=16)
    p.drain(st)
    S.close()
    p.barrier()

def emit_mix(ctx, brFM, w1_in, b1_in, w2_in, b2_in, lt_in, rmsw_in, wo_in, sFM,
             oFM, statsD, probsD):
    p = ctx.p
    xa = ctx.c["xa"]
    ones = ctx.c["ones_col"]
    ones128 = p.sb("ones128", [128, 1], BF16)
    p.op("gpsimd", lambda e: e.memset(ones128[:], 1.0))

    S = ExitStack()
    Sa = ExitStack()
    R1 = p.sb("mixR1", [4, BT], F32, scope=Sa)
    R2 = p.sb("mixR2", [4, BT], F32, scope=Sa)
    R3 = p.sb("mixR3", [4, BT], F32, scope=Sa)
    T4 = p.sb("mixT4", [4, BT], F32, scope=Sa)
    xch = p.sb("mix_xch", [128, BT], BF16, scope=Sa)
    sqt = p.sb("mix_sqt", [128, 512], BF16, scope=Sa)
    abt = p.sb("mix_abt", [128, 512], BF16, scope=Sa)
    psS = p.ps("mix_psS", [4, 512], F32, scope=Sa)
    ld = p.sem()
    mm = p.sem()
    vx = p.sem()
    sc = p.sem()
    gp = p.sem()
    st = p.sem()

    BRS = ("ls", "ll", "do", "vh")
    for bi, bn in enumerate(BRS):
        for Rb in (R1, R2, R3):
            p.op("vector", lambda e, Rb=Rb: e.memset(Rb[:], 0.0),
                 waits=[(gp, gp.n)], inc=vx)
        for co in range(NCH):
            lc = p.op("sync", lambda e, bn=bn, co=co: e.dma_start(
                out=xch[:], in_=brFM[bn][co]),
                waits=[(mm, mm.n), (sc, sc.n)], inc=ld, k=16)
            for n in range(NCH):
                sl = slice(n * 512, (n + 1) * 512)
                s1 = p.op("scalar", lambda e, sl=sl: e.activation(
                    sqt[:], xch[:, sl], AF.Square), waits=[(ld, lc), (mm, mm.n)],
                    inc=sc)
                s2 = p.op("scalar", lambda e, sl=sl: e.activation(
                    abt[:], xch[:, sl], AF.Abs), inc=sc)
                m1 = p.op("tensor", lambda e, co=co, sl=sl: e.matmul(
                    psS[:], ctx.c[f"hmask{co}"][:], xch[:, sl],
                    start=True, stop=True), waits=[(ld, lc), (vx, vx.n)], inc=mm)
                v1 = p.op("vector", lambda e, sl=sl: e.tensor_tensor(
                    R1[:, sl], R1[:, sl], psS[:], ALU.add), waits=[(mm, m1)], inc=vx)
                m2 = p.op("tensor", lambda e, co=co, sl=sl: e.matmul(
                    psS[:], ctx.c[f"hmask{co}"][:], abt[:],
                    start=True, stop=True), waits=[(sc, s2), (vx, v1)], inc=mm)
                v2 = p.op("vector", lambda e, sl=sl: e.tensor_tensor(
                    R2[:, sl], R2[:, sl], psS[:], ALU.add), waits=[(mm, m2)], inc=vx)
                m3 = p.op("tensor", lambda e, co=co, sl=sl: e.matmul(
                    psS[:], ctx.c[f"hmask{co}"][:], sqt[:],
                    start=True, stop=True), waits=[(vx, v2)], inc=mm)
                p.op("vector", lambda e, sl=sl: e.tensor_tensor(
                    R3[:, sl], R3[:, sl], psS[:], ALU.add), waits=[(mm, m3)], inc=vx)
        vmean = p.op("vector", lambda e: e.tensor_scalar(
            R1[:], R1[:], 1.0 / DV, None, ALU.mult), waits=[(vx, vx.n)], inc=vx)
        p.op("gpsimd", lambda e, bi=bi: e.dma_start(
            out=statsD[4 * bi], in_=R1[:]), waits=[(vx, vmean)], inc=ld, k=16)
        vv1 = p.op("vector", lambda e: e.tensor_tensor(T4[:], R1[:], R1[:], ALU.mult),
                   inc=vx)
        vv2 = p.op("vector", lambda e: e.scalar_tensor_tensor(
            T4[:], R3[:], 1.0 / DV, T4[:], ALU.mult, ALU.subtract),
            waits=[(vx, vv1)], inc=vx)
        p.op("gpsimd", lambda e, bi=bi: e.dma_start(
            out=statsD[4 * bi + 1], in_=T4[:]), waits=[(vx, vv2)], inc=ld, k=16)
        vam = p.op("vector", lambda e: e.tensor_scalar(
            R2[:], R2[:], 1.0 / DV, None, ALU.mult), inc=vx)
        p.op("gpsimd", lambda e, bi=bi: e.dma_start(
            out=statsD[4 * bi + 2], in_=R2[:]), waits=[(vx, vam)], inc=ld, k=16)
        sl2 = p.op("scalar", lambda e: e.activation(
            T4[:], R3[:], AF.Sqrt, bias=ctx.constf(0.0, 4)),
            waits=[(vx, vx.n), (ld, ld.n)], inc=sc)
        p.op("gpsimd", lambda e, bi=bi: e.dma_start(
            out=statsD[4 * bi + 3], in_=T4[:]), waits=[(sc, sl2)], inc=ld, k=16)
    stats_done = ld.n
    Sa.close()

    w1a = p.sb("mix_w1a", [128, NCH * 1024], BF16, scope=S)
    w1b = p.sb("mix_w1b", [16, 1024], BF16, scope=S)
    b1c = p.sb("mix_b1c", [128, NCH], F32, scope=S)
    w2c = p.sb("mix_w2c", [128, NCH * 4], BF16, scope=S)
    b2c = p.sb("mix_b2c", [4, 1], F32, scope=S)
    fl4 = p.sb("mix_fl4", [4, 1], F32, scope=S)
    invt = p.sb("mix_invt", [1, 6], F32, scope=S)
    invtb = p.sb("mix_invtb", [1, 1], BF16, scope=S)
    invtrep = p.sb("mix_invtrep", [128, 1], F32, scope=S)
    hgb = p.sb("mix_hgb", [128, NCH * 512], BF16, scope=S)
    stt16 = p.sb("mix_stt16", [16, 512], BF16, scope=S)
    lg4 = p.sb("mix_lg4", [4, 512], F32, scope=S)
    ptile = p.sb("mix_ptile", [4, 512], BF16, scope=S)
    ltm = p.sb("mix_ltm", [128, 4], F32, scope=S)
    entm = p.sb("mix_entm", [128, 4], F32, scope=S)
    mxc = p.sb("mix_mxc", [128, 2], F32, scope=S)
    psG = p.ps("mix_psG", [128, 512], F32, scope=S)
    psL = p.ps("mix_psL", [4, 512], F32, scope=S)
    psT = p.ps("mix_psT", [128, 4], F32, scope=S)
    psB4 = p.ps("mix_psB4", [4, 128], F32, scope=S)

    p.op("gpsimd", lambda e: e.dma_start(
        out=w1a[:], in_=AP(w1_in.tensor, 0, [[1024, 128], [131072, NCH], [1, 1024]])),
        inc=ld, k=16)
    p.op("gpsimd", lambda e: e.dma_start(
        out=w1b[:], in_=AP(w1_in.tensor, 1024 * 1024, [[1024, 16], [1, 1024]])),
        inc=ld, k=16)
    for c in range(NCH):
        p.op("gpsimd", lambda e, c=c: e.dma_start(
            out=b1c[:, c:c + 1], in_=AP(b1_in.tensor, c * 128, [[1, 128], [128, 1]])),
            inc=ld, k=16)
    p.op("gpsimd", lambda e: e.dma_start(
        out=w2c[:], in_=AP(w2_in.tensor, 0, [[4, 128], [512, NCH], [1, 4]])),
        inc=ld, k=16)
    p.op("gpsimd", lambda e: e.dma_start(
        out=b2c[:], in_=AP(b2_in.tensor, 0, [[1, 4], [4, 1]])), inc=ld, k=16)
    p.op("gpsimd", lambda e: e.dma_start(
        out=invt[0:1, 0:1], in_=AP(lt_in.tensor, 0, [[1, 1], [1, 1]])), inc=ld, k=16)
    ldw = ld.n
    p.op("gpsimd", lambda e: e.memset(fl4[:], 0.0), inc=gp)
    p.op("gpsimd", lambda e: e.memset(fl4[0:2, :], 0.08), inc=gp)
    s1 = p.op("scalar", lambda e: e.activation(invt[0:1, 1:2], invt[0:1, 0:1], AF.Abs),
              waits=[(ld, ldw)], inc=sc)
    s2 = p.op("scalar", lambda e: e.activation(
        invt[0:1, 2:3], invt[0:1, 1:2], AF.Exp, scale=-1.0), waits=[(sc, s1)], inc=sc)
    s3 = p.op("scalar", lambda e: e.activation(
        invt[0:1, 3:4], invt[0:1, 2:3], AF.Ln, bias=ctx.constf(1.0, 1)),
        waits=[(sc, s2)], inc=sc)
    v1 = p.op("vector", lambda e: e.tensor_scalar(
        invt[0:1, 4:5], invt[0:1, 0:1], 0.0, None, ALU.max), waits=[(sc, s3)], inc=vx)
    v2 = p.op("vector", lambda e: e.tensor_tensor(
        invt[0:1, 4:5], invt[0:1, 4:5], invt[0:1, 3:4], ALU.add),
        waits=[(vx, v1)], inc=vx)
    v3 = p.op("vector", lambda e: e.tensor_scalar(
        invt[0:1, 4:5], invt[0:1, 4:5], 1e-6, None, ALU.add),
        waits=[(vx, v2)], inc=vx)
    v4 = p.op("vector", lambda e: e.reciprocal(invt[0:1, 5:6], invt[0:1, 4:5]),
              waits=[(vx, v3)], inc=vx)
    v5 = p.op("vector", lambda e: e.tensor_copy(invtb[:], invt[0:1, 5:6]),
              waits=[(vx, v4)], inc=vx)
    tb1 = p.op("tensor", lambda e: e.matmul(psT[:, 0:1], ones[:], invtb[:],
                                            start=True, stop=True),
               waits=[(vx, v5)], inc=mm)
    p.op("vector", lambda e: e.tensor_copy(invtrep[:], psT[:, 0:1]),
         waits=[(mm, tb1)], inc=vx)

    for h in range(H):
        for tn in range(NCH):
            sl = slice(tn * 512, (tn + 1) * 512)
            lst = p.op("sync", lambda e, h=h, tn=tn: e.dma_start(
                out=stt16[:], in_=AP(statsD.tensor, h * BT + tn * 512,
                                     [[4 * BT, 16], [1, 512]])),
                waits=[(ld, stats_done), (mm, mm.n)], inc=ld, k=16)
            for hc in range(NCH):
                for ci in range(NCH):
                    p.op("tensor", lambda e, hc=hc, ci=ci, sl=sl: e.matmul(
                        psG[:], w1a[:, ci * 1024 + hc * 128: ci * 1024 + (hc + 1) * 128],
                        xa[:, ci * BT:(ci + 1) * BT][:, sl],
                        start=(ci == 0), stop=False),
                        waits=([(ld, max(ldw, stats_done)), (sc, sc.n)]
                               if ci == 0 else []), inc=mm)
                mg = p.op("tensor", lambda e, hc=hc: e.matmul(
                    psG[:], w1b[:, hc * 128:(hc + 1) * 128], stt16[:],
                    start=False, stop=True), waits=[(ld, lst)], inc=mm)
                p.op("scalar", lambda e, hc=hc: e.activation(
                    hgb[:, hc * 512:(hc + 1) * 512], psG[:], AF.Gelu,
                    bias=b1c[:, hc:hc + 1]), waits=[(mm, mg)], inc=sc)
            scg = sc.n
            for hc in range(NCH):
                p.op("tensor", lambda e, hc=hc: e.matmul(
                    psL[:], w2c[:, hc * 4:(hc + 1) * 4], hgb[:, hc * 512:(hc + 1) * 512],
                    start=(hc == 0), stop=(hc == NCH - 1)),
                    waits=([(sc, scg), (vx, vx.n)] if hc == 0 else []), inc=mm)
            vl = p.op("vector", lambda e: e.tensor_scalar(
                lg4[:], psL[:], b2c[:], None, ALU.add), waits=[(mm, mm.n)], inc=vx)
            for su in range(4):
                t1 = p.op("tensor", lambda e, su=su: e.transpose(
                    psT[:], lg4[:, su * 128:(su + 1) * 128],
                    ctx.c["ident_f"][0:4, 0:4]), waits=[(vx, vx.n)], inc=mm)
                vt = p.op("vector", lambda e: e.tensor_copy(ltm[:], psT[:]),
                          waits=[(mm, t1)], inc=vx)
                vm = p.op("vector", lambda e: e.reduce_max(
                    mxc[:, 0:1], ltm[:], axis=AX.X), waits=[(vx, vt)], inc=vx)
                vm2 = p.op("vector", lambda e: e.tensor_scalar(
                    mxc[:, 1:2], mxc[:, 0:1], -1.0, invtrep[:], ALU.mult, ALU.mult),
                    waits=[(vx, vm)], inc=vx)
                se = p.op("scalar", lambda e: e.activation(
                    entm[:], ltm[:], AF.Exp, scale=invtrep[:],
                    bias=mxc[:, 1:2]),
                    waits=[(vx, vm2)], inc=sc)
                vsum = p.op("vector", lambda e: e.reduce_sum(
                    mxc[:, 0:1], entm[:], axis=AX.X), waits=[(sc, se)], inc=vx)
                vr = p.op("vector", lambda e: e.reciprocal(mxc[:, 0:1], mxc[:, 0:1]),
                          waits=[(vx, vsum)], inc=vx)
                vp = p.op("vector", lambda e: e.tensor_scalar(
                    entm[:], entm[:], mxc[:, 0:1], None, ALU.mult),
                    waits=[(vx, vr)], inc=vx)
                t2 = p.op("tensor", lambda e: e.transpose(
                    psB4[:], entm[:], ctx.c["ident_f"][:]), waits=[(vx, vp)], inc=mm)
                vf = p.op("vector", lambda e, su=su: e.tensor_scalar(
                    ptile[:, su * 128:(su + 1) * 128],
                    psB4[:], 1.0 - 2.0 * 0.08, fl4[:], ALU.mult, ALU.add),
                    waits=[(mm, t2), (gp, gp.n)] + ([(ld, ld.n)] if su == 0 else []),
                    inc=vx)
            vc = p.op("vector", lambda e, h=h: e.tensor_scalar(
                ptile[0:1, :], ptile[0:1, :],
                ctx.c["beta"]["crlrep"][0:1, h:h + 1], None, ALU.add),
                waits=[(vx, vx.n)], inc=vx)
            p.op("gpsimd", lambda e, h=h, tn=tn, vc=vc: e.dma_start(
                out=AP(probsD.tensor, h * 4 * BT + tn * 512, [[BT, 4], [1, 512]]),
                in_=ptile[:]), waits=[(vx, vc)], inc=ld, k=16)
    p.drain(st)
    S.close()
    p.barrier()

    S2 = ExitStack()
    brb = p.sb("mix_brb", [128, 8 * 512], BF16, scope=S2)
    prow4 = p.sb("mix_prow4", [1, 4 * 512], BF16, scope=S2)
    acc = [p.sb(f"mix_acc{c2}", [128, 512], F32, scope=S2) for c2 in range(2)]
    tmpm = p.sb("mix_tmpm", [128, 512], F32, scope=S2)
    sqm = p.sb("mix_sqm", [128, 512], BF16, scope=S2)
    rrow = p.sb("mix_rrow", [1, 512], F32, scope=S2)
    rrowb = p.sb("mix_rrowb", [1, 512], BF16, scope=S2)
    rmsw = p.sb("mix_rmsw", [128, 2], F32, scope=S2)
    obf2 = p.sb("mix_obf2", [128, 1024], BF16, scope=S2)
    st2 = p.sem()
    psB = p.ps("mix_psB", [128, 512], F32, scope=S2)
    psR = p.ps("mix_psR", [1, 512], F32, scope=S2)
    ld2 = p.sem()
    lg2 = p.sem()
    for c2 in range(2):
        p.op("gpsimd", lambda e, c2=c2: e.dma_start(
            out=rmsw[:, c2:c2 + 1],
            in_=AP(rmsw_in.tensor, c2 * 128, [[1, 128], [128, 1]])), inc=lg2, k=16)
    for h in range(H):
        for tn in range(NCH):
            sl = slice(tn * 512, (tn + 1) * 512)
            b_ = tn // 4
            lb = ld2.n
            for bi, bn in enumerate(BRS):
                p.op("sync", lambda e, h=h, tn=tn, bi=bi: e.dma_start(
                    out=prow4[0:1, bi * 512:(bi + 1) * 512],
                    in_=AP(probsD.tensor, h * 4 * BT + bi * BT + tn * 512,
                           [[512, 1], [1, 512]])),
                    waits=[(mm, mm.n)], inc=ld2, k=16)
                for c2 in range(2):
                    lb = p.op("sync", lambda e, bn=bn, h=h, c2=c2, sl=sl, bi=bi:
                              e.dma_start(out=brb[:, (bi * 2 + c2) * 512:
                                                   (bi * 2 + c2 + 1) * 512],
                                          in_=brFM[bn][2 * h + c2][:, sl]),
                              waits=[(vx, vx.n), (sc, sc.n)], inc=ld2, k=16)
            for c2 in range(2):
                for bi in range(4):
                    tb2 = p.op("tensor", lambda e, bi=bi: e.matmul(
                        psB[:], ones[:], prow4[0:1, bi * 512:(bi + 1) * 512],
                        start=True, stop=True),
                        waits=[(ld2, lb), (vx, vx.n)], inc=mm)
                    if bi == 0:
                        p.op("vector", lambda e, c2=c2, bi=bi: e.tensor_tensor(
                            acc[c2][:], brb[:, (bi * 2 + c2) * 512:(bi * 2 + c2 + 1) * 512],
                            psB[:], ALU.mult), waits=[(mm, tb2)], inc=vx)
                    else:
                        p.op("vector", lambda e, c2=c2, bi=bi: e.tensor_tensor(
                            tmpm[:], brb[:, (bi * 2 + c2) * 512:(bi * 2 + c2 + 1) * 512],
                            psB[:], ALU.mult), waits=[(mm, tb2)], inc=vx)
                        p.op("vector", lambda e, c2=c2: e.tensor_add(
                            acc[c2][:], acc[c2][:], tmpm[:]), inc=vx)
            accv = vx.n
            s1 = p.op("scalar", lambda e: e.activation(sqm[:], acc[0][:], AF.Square),
                      waits=[(vx, accv)], inc=sc)
            m1 = p.op("tensor", lambda e: e.matmul(psR[:], ones128[:], sqm[:],
                                                   start=True, stop=False),
                      waits=[(sc, s1)], inc=mm)
            s2 = p.op("scalar", lambda e: e.activation(sqm[:], acc[1][:], AF.Square),
                      waits=[(mm, m1)], inc=sc)
            m2 = p.op("tensor", lambda e: e.matmul(psR[:], ones128[:], sqm[:],
                                                   start=False, stop=True),
                      waits=[(sc, s2)], inc=mm)
            s3 = p.op("scalar", lambda e: e.activation(
                rrow[:], psR[:], AF.Ln, scale=1.0 / DV, bias=ctx.constf(1e-5, 1)),
                waits=[(mm, m2)], inc=sc)
            s4 = p.op("scalar", lambda e: e.activation(
                rrow[:], rrow[:], AF.Exp, scale=-0.5), waits=[(sc, s3)], inc=sc)
            v1 = p.op("vector", lambda e: e.tensor_copy(rrowb[:], rrow[:]),
                      waits=[(sc, s4)], inc=vx)
            m3 = p.op("tensor", lambda e: e.matmul(psB[:], ones[:], rrowb[:],
                                                   start=True, stop=True),
                      waits=[(vx, v1)], inc=mm)
            for c2 in range(2):
                v2 = p.op("vector", lambda e, c2=c2: e.tensor_tensor(
                    acc[c2][:], acc[c2][:], psB[:], ALU.mult), waits=[(mm, m3)], inc=vx)
                sco = p.op("scalar", lambda e, c2=c2: e.activation(
                    obf2[:, c2 * 512:(c2 + 1) * 512], acc[c2][:], AF.Identity,
                    scale=rmsw[:, c2:c2 + 1]),
                    waits=[(vx, v2), (lg2, 32), (st2, st2.n)], inc=sc)
                p.op("sync", lambda e, c2=c2, h=h, sl=sl, sco=sco: e.dma_start(
                    out=oFM[2 * h + c2][:, sl], in_=obf2[:, c2 * 512:(c2 + 1) * 512]),
                    waits=[(sc, sco)], inc=st2, k=16)
    p.drain(st2)
    S2.close()
    p.barrier()

    S3 = ExitStack()
    wov = p.sb("mix_wo", [128, NCH * 1024], BF16, scope=S3)
    osl = p.sb("mix_osl", [128, NCH * 512], BF16, scope=S3)
    sbf = p.sb("mix_sbf", [128, NCH * 512], BF16, scope=S3)
    psA = p.ps("mix_psA", [128, 512], F32, scope=S3)
    ld3 = p.sem()
    st3 = p.sem()
    p.op("gpsimd", lambda e: e.dma_start(
        out=wov[:], in_=AP(wo_in.tensor, 0, [[1024, 128], [131072, NCH], [1, 1024]])),
        inc=ld3, k=16)
    for n in range(NCH):
        for ci in range(NCH):
            p.op("sync", lambda e, n=n, ci=ci: e.dma_start(
                out=osl[:, ci * 512:(ci + 1) * 512],
                in_=oFM[ci][:, n * 512:(n + 1) * 512]),
                waits=[(mm, mm.n)], inc=ld3, k=16)
        ldn = ld3.n
        for co in range(NCH):
            for ci in range(NCH):
                p.op("tensor", lambda e, co=co, ci=ci: e.matmul(
                    psA[:], wov[:, ci * 1024 + co * 128: ci * 1024 + (co + 1) * 128],
                    osl[:, ci * 512:(ci + 1) * 512],
                    start=(ci == 0), stop=(ci == NCH - 1)),
                    waits=([(ld3, ldn), (vx, vx.n)] if ci == 0 else []), inc=mm)
            p.op("vector", lambda e, co=co: e.tensor_copy(
                sbf[:, co * 512:(co + 1) * 512], psA[:]),
                waits=[(mm, mm.n)] + ([(st3, st3.n)] if co == 0 else []), inc=vx)
        for co in range(NCH):
            p.op("sync", lambda e, co=co, n=n: e.dma_start(
                out=sFM[co][:, n * 512:(n + 1) * 512],
                in_=sbf[:, co * 512:(co + 1) * 512]),
                waits=[(vx, vx.n)], inc=st3, k=16)
    p.drain(st3)
    S3.close()
    p.barrier()


def emit_wquant_gu(ctx, w_in, wqT):
    p = ctx.p
    R, C = w_in.shape[0], w_in.shape[1]
    NR = R // 128
    NCC = C // 128
    p._id += 1
    u = f"_{p._id}"
    S = ExitStack()
    wt = p.sb("wq_wt" + u, [128, C], F32, scope=S)
    wi = p.sb("wq_wi" + u, [128, C], I32, scope=S)
    wb = p.sb("wq_wb" + u, [128, C], BF16, scope=S)
    mx = p.sb("wq_mx" + u, [128, 4], F32, scope=S)
    tbuf = p.sb("wq_tb" + u, [128, NCC * 128], BF16, scope=S)
    pst = p.ps("wq_ps" + u, [128, 128], BF16, scope=S)
    ld = p.sem()
    mm = p.sem()
    vx = p.sem()
    sc = p.sem()
    st = p.sem()
    for ic in range(NR):
        l1 = p.op("sync", lambda e, ic=ic: e.dma_start(
            out=wt[:], in_=AP(w_in.tensor, ic * 128 * C, [[C, 128], [1, C]])),
            waits=[(vx, vx.n), (mm, mm.n)], inc=ld, k=16)
        v1 = p.op("vector", lambda e: e.reduce_max(
            mx[:, 0:1], wt[:], axis=AX.X, apply_absolute_value=True),
            waits=[(ld, l1)], inc=vx)
        v2 = p.op("vector", lambda e: e.reciprocal(mx[:, 1:2], mx[:, 0:1]),
                  waits=[(vx, v1)], inc=vx)
        v3 = p.op("vector", lambda e: e.tensor_scalar(
            mx[:, 2:3], mx[:, 1:2], 127.0, None, ALU.mult), waits=[(vx, v2)], inc=vx)
        v4 = p.op("vector", lambda e: e.tensor_scalar(
            mx[:, 3:4], mx[:, 0:1], 1.0 / 127.0, None, ALU.mult),
            waits=[(vx, v3)], inc=vx)
        v5 = p.op("vector", lambda e: e.tensor_scalar(
            wi[:], wt[:], mx[:, 2:3], None, ALU.mult), waits=[(vx, v4)], inc=vx)
        v6 = p.op("vector", lambda e: e.tensor_scalar(
            wb[:], wi[:], mx[:, 3:4], None, ALU.mult), waits=[(vx, v5)], inc=vx)
        for dc in range(NCC):
            t1 = p.op("tensor", lambda e, dc=dc: e.transpose(
                pst[:], wb[:, dc * 128:(dc + 1) * 128], ctx.c["ident_bf"][:]),
                waits=[(vx, vx.n)], inc=mm)
            p.op("vector", lambda e, dc=dc: e.tensor_copy(
                tbuf[:, dc * 128:(dc + 1) * 128], pst[:]), waits=[(mm, t1)], inc=vx)
        p.op("sync", lambda e, ic=ic: e.dma_start(
            out=AP(wqT.tensor, ic * 128, [[R, 128], [128 * R, NCC], [1, 128]]),
            in_=tbuf[:].rearrange("p (dc j) -> p dc j", j=128)),
            waits=[(vx, vx.n)], inc=st, k=16)
        p.op("vector", lambda e: e.engine_nop(), waits=[(st, st.n)], inc=vx)
    p.drain(st)
    S.close()
    p.barrier()


def emit_ln_quant(ctx, x_in, mv_in, clip_in, xqa, use_sig=True):
    import os as _os
    _lp = int(_os.environ.get("LNQ", "3"))
    """LN(x) then act-fake-quant with global max from mv_in [1,8] partials.
    Writes xqa [128, NCH*BT] bf16 (FM). use_sig: scale=sigmoid(clip)*gmax/127."""
    p = ctx.p
    S = ExitStack()
    xbuf = p.sb("lq_x", [128, D], F32, scope=S)
    sm = p.sb("lq_sm", [128, NTT], F32, scope=S)
    sq = p.sb("lq_sq", [128, NTT], F32, scope=S)
    rstd = p.sb("lq_rstd", [128, NTT], F32, scope=S)
    negmr = p.sb("lq_negmr", [128, NTT], F32, scope=S)
    tmp = p.sb("lq_tmp", [128, NTT], F32, scope=S)
    junk = p.sb("lq_junk", [128, D], F32, scope=S)
    xnt = p.sb("lq_xnt", [128, D], F32, scope=S)
    xit = p.sb("lq_xit", [128, D], I32, scope=S)
    xbt = p.sb("lq_xbt", [128, D], BF16, scope=S)
    gl = p.sb("lq_gl", [1, 8], F32, scope=S)
    glb = p.sb("lq_glb", [1, 1], BF16, scope=S)
    rep = p.sb("lq_rep", [128, 2], F32, scope=S)
    pst = p.ps("lq_ps", [128, 128], F32, scope=S)
    ld = p.sem()
    lg = p.sem()
    mm = p.sem()
    vx = p.sem()
    sc = p.sem()

    if _lp < 3:
        p.op("vector", lambda e: e.memset(rep[:], 1.0), inc=vx)
    scale_done = vx.n
    if _lp >= 3:
      l1 = p.op("gpsimd", lambda e: e.dma_start(
          out=gl[0:1, 0:8], in_=AP(mv_in.tensor, 0, [[8, 1], [1, 8]])), inc=ld, k=16)
      l2 = p.op("gpsimd", lambda e: e.dma_start(
          out=tmp[0:1, 4:5], in_=AP(clip_in.tensor, 0, [[1, 1], [1, 1]])),
          inc=ld, k=16)
      v1 = p.op("vector", lambda e: e.reduce_max(
          tmp[0:1, 0:1], gl[0:1, 0:8], axis=AX.X), waits=[(ld, 32)], inc=vx)
      s1 = p.op("scalar", lambda e: e.activation(
          tmp[0:1, 1:2], tmp[0:1, 4:5], AF.Sigmoid), waits=[(ld, 32)], inc=sc)
      v2 = p.op("vector", lambda e: e.tensor_tensor(
          tmp[0:1, 2:3], tmp[0:1, 0:1], tmp[0:1, 1:2], ALU.mult),
          waits=[(vx, v1), (sc, s1)], inc=vx)
      v3 = p.op("vector", lambda e: e.tensor_scalar(
          tmp[0:1, 2:3], tmp[0:1, 2:3], 1.0 / 127.0, None, ALU.mult),
          waits=[(vx, v2)], inc=vx)
      v4 = p.op("vector", lambda e: e.reciprocal(tmp[0:1, 3:4], tmp[0:1, 2:3]),
                waits=[(vx, v3)], inc=vx)
      v5 = p.op("vector", lambda e: e.tensor_copy(glb[:], tmp[0:1, 2:3]),
                waits=[(vx, v4)], inc=vx)
      t1 = p.op("tensor", lambda e: e.matmul(pst[:, 0:1], ctx.c["ones_col"][:], glb[:],
                                             start=True, stop=True),
                waits=[(vx, v5)], inc=mm)
      v6 = p.op("vector", lambda e: e.tensor_copy(rep[:, 0:1], pst[:, 0:1]),
                waits=[(mm, t1)], inc=vx)
      v7 = p.op("vector", lambda e: e.tensor_copy(glb[:], tmp[0:1, 3:4]),
                waits=[(vx, v6)], inc=vx)
      t2 = p.op("tensor", lambda e: e.matmul(pst[:, 0:1], ctx.c["ones_col"][:], glb[:],
                                             start=True, stop=True),
                waits=[(vx, v7)], inc=mm)
      v8 = p.op("vector", lambda e: e.tensor_copy(rep[:, 1:2], pst[:, 0:1]),
                waits=[(mm, t2)], inc=vx)
      scale_done = v8

    x_t = x_in.rearrange("b (jt p) d -> (b jt) p d", p=128)
    for i in range(NTT):
        li = p.op("sync", lambda e, i=i: e.dma_start(out=xbuf[:], in_=x_t[i]),
                  waits=[(sc, sc.n), (vx, vx.n)], inc=ld, k=16)
        s_i = p.op("scalar", lambda e, i=i: e.activation(
            junk[:], xbuf[:], AF.Square), waits=[(ld, li), (vx, vx.n)], inc=sc)
        p.op("vector", lambda e, i=i: e.reduce_sum(
            sq[:, i:i + 1], junk[:], axis=AX.X), waits=[(sc, s_i)], inc=vx)
        p.op("vector", lambda e, i=i: e.reduce_sum(
            sm[:, i:i + 1], xbuf[:], axis=AX.X),
            waits=[(ld, li)], inc=vx)
    w1_ = p.op("vector", lambda e: e.tensor_scalar(sm[:], sm[:], 1.0 / D, None, ALU.mult),
               waits=[(vx, vx.n), (sc, sc.n)], inc=vx)
    w2_ = p.op("vector", lambda e: e.tensor_scalar(sq[:], sq[:], 1.0 / D, None, ALU.mult),
               waits=[(vx, w1_)], inc=vx)
    w3_ = p.op("vector", lambda e: e.tensor_tensor(tmp[:], sm[:], sm[:], ALU.mult),
               waits=[(vx, w2_)], inc=vx)
    w4_ = p.op("vector", lambda e: e.tensor_sub(sq[:], sq[:], tmp[:]),
               waits=[(vx, w3_)], inc=vx)
    w5_ = p.op("scalar", lambda e: e.activation(rstd[:], sq[:], AF.Sqrt,
                                                bias=ctx.constf(1e-5)),
               waits=[(vx, w4_)], inc=sc)
    w6_ = p.op("vector", lambda e: e.reciprocal(rstd[:], rstd[:]),
               waits=[(sc, w5_)], inc=vx)
    w7_ = p.op("vector", lambda e: e.scalar_tensor_tensor(
        negmr[:], sm[:], -1.0, rstd[:], ALU.mult, ALU.mult),
        waits=[(vx, w6_)], inc=vx)
    stats_done = w7_

    for i in range(NTT if _lp >= 2 else 0):
        li = p.op("sync", lambda e, i=i: e.dma_start(out=xbuf[:], in_=x_t[i]),
                  waits=[(sc, sc.n)], inc=ld, k=16)
        si = p.op("scalar", lambda e, i=i: e.activation(
            xnt[:], xbuf[:], AF.Identity,
            bias=negmr[:, i:i + 1], scale=rstd[:, i:i + 1]),
            waits=[(vx, max(stats_done, scale_done)), (mm, mm.n), (ld, li)], inc=sc)
        q1 = p.op("vector", lambda e: e.tensor_scalar(
            xit[:], xnt[:], rep[:, 1:2], None, ALU.mult), waits=[(sc, si)], inc=vx)
        q1b = p.op("vector", lambda e: e.tensor_copy(xnt[:], xit[:]),
                   waits=[(vx, q1)], inc=vx)
        q2 = p.op("vector", lambda e: e.tensor_scalar(
            xnt[:], xnt[:], -128.0, 127.0, ALU.max, ALU.min),
            waits=[(vx, q1b)], inc=vx)
        q3 = p.op("vector", lambda e: e.tensor_scalar(
            xbt[:], xnt[:], rep[:, 0:1], None, ALU.mult), waits=[(vx, q2)], inc=vx)
        for c in range(NCH):
            tt = p.op("tensor", lambda e, c=c: e.transpose(
                pst.bitcast(BF16)[:, 0:128], xbt[:, c * 128:(c + 1) * 128],
                ctx.c["ident_bf"][:]), waits=[(vx, vx.n)], inc=mm)
            p.op("vector", lambda e, c=c, i=i: e.tensor_copy(
                xqa[:, c * BT + i * 128: c * BT + (i + 1) * 128],
                pst.bitcast(BF16)[:, 0:128]), waits=[(mm, tt)], inc=vx)
    S.close()
    p.barrier()


def emit_ffn0(ctx, xqa, wgT, wuT, wdT, m2v_in, clip_d_in, gFM, m2_part, phase):
    p = ctx.p
    NI = 2816 // 128
    S = ExitStack()
    wg8 = p.sb("f0_wg8", [128, NCH * 128], BF16, scope=S)
    wu8 = p.sb("f0_wu8", [128, NCH * 128], BF16, scope=S)
    wd8 = p.sb("f0_wd8", [128, 1024], BF16, scope=S)
    gb = p.sb("f0_gb", [128, 256], F32, scope=S)
    hf = p.sb("f0_hf", [128, 256], F32, scope=S)
    hq = p.sb("f0_hq", [128, 256], BF16, scope=S)
    hi = p.sb("f0_hi", [128, 256], I32, scope=S)
    mx = p.sb("f0_mx", [128, 2], F32, scope=S)
    m1o = p.sb("f0_m1o", [1, 130], F32, scope=S)
    rep = p.sb("f0_rep", [128, 2], F32, scope=S)
    glb = p.sb("f0_glb", [1, 1], BF16, scope=S)
    gout = p.sb("f0_gout", [128, 256], BF16, scope=S)
    psGU = p.ps("f0_psGU", [128, 512], F32, scope=S)
    psG = psGU[:, 0:256]
    psU = psGU[:, 256:512]
    if phase == "b":
        psD2 = [p.ps(f"f0_psD{d}", [128, 512], F32, scope=S) for d in range(4)]
        psD = [psD2[d // 2][:, (d % 2) * 256:(d % 2) * 256 + 256] for d in range(NCH)]
    mxD = ctx.d["scr128"]
    ld = p.sem()
    lg = p.sem()
    mm = p.sem()
    mmd = p.sem()
    vx = p.sem()
    sc = p.sem()
    st = p.sem()

    if phase == "b":
        gl = p.sb("f0_gl", [1, 8], F32, scope=S)
        tmp1 = p.sb("f0_tmp1", [1, 4], F32, scope=S)
        l1 = p.op("gpsimd", lambda e: e.dma_start(
            out=gl[0:1, 0:8], in_=AP(m2v_in.tensor, 0, [[8, 1], [1, 8]])), inc=lg, k=16)
        l2 = p.op("gpsimd", lambda e: e.dma_start(
            out=tmp1[0:1, 3:4], in_=AP(clip_d_in.tensor, 0, [[1, 1], [1, 1]])),
            inc=lg, k=16)
        v1 = p.op("vector", lambda e: e.reduce_max(
            tmp1[0:1, 0:1], gl[0:1, 0:8], axis=AX.X), waits=[(lg, 32)], inc=vx)
        s1 = p.op("scalar", lambda e: e.activation(
            tmp1[0:1, 1:2], tmp1[0:1, 3:4], AF.Sigmoid), waits=[(lg, 32)], inc=sc)
        v2 = p.op("vector", lambda e: e.tensor_tensor(
            tmp1[0:1, 2:3], tmp1[0:1, 0:1], tmp1[0:1, 1:2], ALU.mult),
            waits=[(vx, v1), (sc, s1)], inc=vx)
        v3 = p.op("vector", lambda e: e.tensor_scalar(
            tmp1[0:1, 2:3], tmp1[0:1, 2:3], 1.0 / 127.0, None, ALU.mult),
            waits=[(vx, v2)], inc=vx)
        v4 = p.op("vector", lambda e: e.reciprocal(tmp1[0:1, 3:4], tmp1[0:1, 2:3]),
                  waits=[(vx, v3)], inc=vx)
        v5 = p.op("vector", lambda e: e.tensor_copy(glb[:], tmp1[0:1, 2:3]),
                  waits=[(vx, v4)], inc=vx)
        t1 = p.op("tensor", lambda e: e.matmul(
            psGU[:, 0:1], ctx.c["ones_col"][:], glb[:], start=True, stop=True),
            waits=[(vx, v5)], inc=mm)
        v6 = p.op("vector", lambda e: e.tensor_copy(rep[:, 0:1], psGU[:, 0:1]),
                  waits=[(mm, t1)], inc=vx)
        v7 = p.op("vector", lambda e: e.tensor_copy(glb[:], tmp1[0:1, 3:4]),
                  waits=[(vx, v6)], inc=vx)
        t2 = p.op("tensor", lambda e: e.matmul(
            psGU[:, 0:1], ctx.c["ones_col"][:], glb[:], start=True, stop=True),
            waits=[(vx, v7)], inc=mm)
        v8 = p.op("vector", lambda e: e.tensor_copy(rep[:, 1:2], psGU[:, 0:1]),
                  waits=[(mm, t2)], inc=vx)
    p.op("vector", lambda e: e.memset(mx[:], 0.0), inc=vx)
    setup_done = vx.n

    NN = BT // 256
    for n in range(NN):
        sl = slice(n * 256, (n + 1) * 256)
        for ic in range(NI):
            lw = p.op("sync", lambda e, ic=ic: e.dma_start(
                out=wg8[:], in_=AP(wgT.tensor, ic * 128,
                                   [[2816, 128], [128 * 2816, NCH], [1, 128]])),
                waits=[(mm, mm.n)], inc=ld, k=16)
            lw2 = p.op("sync", lambda e, ic=ic: e.dma_start(
                out=wu8[:], in_=AP(wuT.tensor, ic * 128,
                                   [[2816, 128], [128 * 2816, NCH], [1, 128]])),
                inc=ld, k=16)
            for dc in range(NCH):
                p.op("tensor", lambda e, dc=dc, sl=sl: e.matmul(
                    psG, wg8[:, dc * 128:(dc + 1) * 128],
                    xqa[:, dc * BT:(dc + 1) * BT][:, sl],
                    start=(dc == 0), stop=(dc == NCH - 1)),
                    waits=([(ld, lw2), (sc, sc.n), (vx, vx.n)]
                           if dc == 0 else []), inc=mm)
            for dc in range(NCH):
                p.op("tensor", lambda e, dc=dc, sl=sl: e.matmul(
                    psU, wu8[:, dc * 128:(dc + 1) * 128],
                    xqa[:, dc * BT:(dc + 1) * BT][:, sl],
                    start=(dc == 0), stop=(dc == NCH - 1)), inc=mm)
            mgu = mm.n
            sg = p.op("scalar", lambda e: e.activation(gb[:], psG, AF.Silu),
                      waits=[(mm, mgu), (vx, vx.n)], inc=sc)
            vh_ = p.op("vector", lambda e: e.tensor_tensor(
                hf[:], gb[:], psU, ALU.mult), waits=[(sc, sg)], inc=vx)
            if phase == "a":
                vm = p.op("vector", lambda e: e.reduce_max(
                    mx[:, 1:2], hf[:], axis=AX.X, apply_absolute_value=True),
                    waits=[(vx, vh_)], inc=vx)
                p.op("vector", lambda e: e.tensor_max(mx[:, 0:1], mx[:, 0:1],
                                                      mx[:, 1:2]),
                     waits=[(vx, vm)], inc=vx)
            else:
                q1 = p.op("vector", lambda e: e.tensor_scalar(
                    hi[:], hf[:], rep[:, 1:2], None, ALU.mult),
                    waits=[(vx, vh_)], inc=vx)
                q1b = p.op("vector", lambda e: e.tensor_copy(hf[:], hi[:]),
                           waits=[(vx, q1)], inc=vx)
                q2 = p.op("vector", lambda e: e.tensor_scalar(
                    hf[:], hf[:], -128.0, 127.0, ALU.max, ALU.min),
                    waits=[(vx, q1b)], inc=vx)
                q3 = p.op("vector", lambda e: e.tensor_scalar(
                    hq[:], hf[:], rep[:, 0:1], None, ALU.mult),
                    waits=[(vx, q2)], inc=vx)
                ldd = p.op("sync", lambda e, ic=ic: e.dma_start(
                    out=wd8[:],
                    in_=AP(wdT.tensor, ic * 128 * 1024, [[1024, 128], [1, 1024]])),
                    waits=[(mmd, mmd.n)], inc=ld, k=16)
                for dc in range(NCH):
                    p.op("tensor", lambda e, dc=dc, ic=ic: e.matmul(
                        psD[dc], wd8[:, dc * 128:(dc + 1) * 128], hq[:],
                        start=(ic == 0 and dc % 2 == 0), stop=(ic == NI - 1)),
                        waits=([(vx, q3), (ld, ldd), (sc, sc.n)]
                               if dc == 0 else []), inc=mmd)
        if phase == "b":
            for dc in range(NCH):
                sg2 = p.op("scalar", lambda e, dc=dc: e.activation(
                    gout[:], psD[dc], AF.Silu), waits=[(mmd, mmd.n)], inc=sc)
                p.op("sync", lambda e, dc=dc, sl=sl, sg2=sg2: e.dma_start(
                    out=gFM[dc][:, sl], in_=gout[:]), waits=[(sc, sg2)], inc=st, k=16)
                p.op("scalar", lambda e: e.nop(), waits=[(st, st.n)], inc=sc)
    if phase == "a":
        va = p.op("gpsimd", lambda e: e.dma_start(out=mxD, in_=mx[:, 0:1]),
                  waits=[(vx, vx.n)], inc=lg, k=16)
        vb = p.op("gpsimd", lambda e: e.dma_start(
            out=m1o[0:1, 0:128], in_=mxD.rearrange("p o -> o p")),
            waits=[(lg, va)], inc=lg, k=16)
        vc = p.op("vector", lambda e: e.reduce_max(
            m1o[0:1, 128:129], m1o[0:1, 0:128], axis=AX.X), waits=[(lg, vb)], inc=vx)
        p.op("sync", lambda e: e.dma_start(out=m2_part, in_=m1o[0:1, 128:129]),
             waits=[(vx, vx.n)], inc=st, k=16)
    p.drain(st)
    S.close()
    p.barrier()


def emit_combine_ln1(ctx, x_in, sFM, gFM, x1_out, xn1FM, m4_part,
                     lng_in, lnb_in, xnfD):
    p = ctx.p
    S = ExitStack()
    sg = p.sb("cb_sg", [128, D], BF16, scope=S)
    gg = p.sb("cb_gg", [128, D], BF16, scope=S)
    xt = p.sb("cb_xt", [128, D], F32, scope=S)
    sgt = p.sb("cb_sgt", [128, D], F32, scope=S)
    xnt = p.sb("cb_xnt", [128, D], F32, scope=S)
    xbt = p.sb("cb_xbt", [128, 128], BF16, scope=S)
    st4 = p.sb("cb_st4", [128, 16], F32, scope=S)
    lngr = p.sb("cb_lngr", [128, D], F32, scope=S)
    lnbr = p.sb("cb_lnbr", [128, D], F32, scope=S)
    lrow = p.sb("cb_lrow", [1, D], BF16, scope=S)
    mx = p.sb("cb_mx", [128, 2], F32, scope=S)
    m1o = p.sb("cb_m1o", [1, 130], F32, scope=S)
    pst = p.ps("cb_ps", [128, 128], F32, scope=S)
    psB = p.ps("cb_psB", [128, 512], F32, scope=S)
    mxD = ctx.d["scr128"]
    ld = p.sem()
    lg = p.sem()
    mm = p.sem()
    vx = p.sem()
    sc = p.sem()
    st = p.sem()

    for (src_in, dst) in ((lng_in, lngr), (lnb_in, lnbr)):
        l1 = p.op("gpsimd", lambda e, src_in=src_in: e.dma_start(
            out=lrow[:], in_=AP(src_in.tensor, 0, [[D, 1], [1, D]])),
            waits=[(vx, vx.n)], inc=lg, k=16)
        for nn in range(2):
            t1 = p.op("tensor", lambda e, nn=nn: e.matmul(
                psB[:], ctx.c["ones_col"][:], lrow[0:1, nn * 512:(nn + 1) * 512],
                start=True, stop=True), waits=[(lg, l1), (vx, vx.n)], inc=mm)
            p.op("vector", lambda e, nn=nn, dst=dst: e.tensor_copy(
                dst[:, nn * 512:(nn + 1) * 512], psB[:]), waits=[(mm, t1)], inc=vx)
    p.op("vector", lambda e: e.memset(mx[:], 0.0), inc=vx)
    rep_done = vx.n

    x_t = x_in.rearrange("b (jt p) d -> (b jt) p d", p=128)
    x1_t = x1_out.rearrange("b (jt p) d -> (b jt) p d", p=128)
    for i in range(NTT):
        b, jt = i // NT, i % NT
        lx = p.op("sync", lambda e, i=i: e.dma_start(out=xt[:], in_=x_t[i]),
                  waits=[(vx, vx.n), (sc, sc.n), (st, st.n)], inc=ld, k=16)
        for c in range(NCH):
            p.op("sync", lambda e, c=c, b=b, jt=jt: e.dma_start(
                out=sg[:, c * 128:(c + 1) * 128],
                in_=sFM[c][:, b * L + jt * 128: b * L + (jt + 1) * 128]),
                waits=([(mm, mm.n), (vx, vx.n)] if c == 0 else []), inc=ld, k=16)
            p.op("sync", lambda e, c=c, b=b, jt=jt: e.dma_start(
                out=gg[:, c * 128:(c + 1) * 128],
                in_=gFM[c][:, b * L + jt * 128: b * L + (jt + 1) * 128]), inc=ld, k=16)
        lsg = ld.n
        v1 = p.op("vector", lambda e: e.tensor_mul(sg[:], sg[:], gg[:]),
                  waits=[(ld, lsg)], inc=vx)
        for c in range(NCH):
            t1 = p.op("tensor", lambda e, c=c: e.transpose(
                pst.bitcast(BF16)[:, 0:128], sg[:, c * 128:(c + 1) * 128],
                ctx.c["ident_bf"][:]), waits=[(vx, vx.n)], inc=mm)
            p.op("vector", lambda e, c=c: e.tensor_copy(
                sgt[:, c * 128:(c + 1) * 128],
                pst.bitcast(BF16)[:, 0:128]), waits=[(mm, t1)], inc=vx)
        v2 = p.op("vector", lambda e: e.tensor_add(xt[:], xt[:], sgt[:]),
                  waits=[(vx, vx.n), (ld, lx)], inc=vx)
        p.op("sync", lambda e, i=i: e.dma_start(out=x1_t[i], in_=xt[:]),
             waits=[(vx, v2)], inc=st, k=16)
        s1 = p.op("scalar", lambda e: e.activation(
            sgt[:], xt[:], AF.Square),
            waits=[(vx, v2)], inc=sc)
        v2b = p.op("vector", lambda e: e.reduce_sum(st4[:, 0:1], sgt[:], axis=AX.X),
                   waits=[(sc, s1)], inc=vx)
        v3 = p.op("vector", lambda e: e.reduce_sum(st4[:, 1:2], xt[:], axis=AX.X),
                  waits=[(vx, v2)], inc=vx)
        v4 = p.op("vector", lambda e: e.tensor_scalar(
            st4[:, 1:2], st4[:, 1:2], 1.0 / D, None, ALU.mult),
            waits=[(vx, v3)], inc=vx)
        v5 = p.op("vector", lambda e: e.tensor_tensor(
            st4[:, 2:3], st4[:, 1:2], st4[:, 1:2], ALU.mult),
            waits=[(vx, v4)], inc=vx)
        v6 = p.op("vector", lambda e: e.scalar_tensor_tensor(
            st4[:, 2:3], st4[:, 0:1], 1.0 / D, st4[:, 2:3], ALU.mult, ALU.subtract),
            waits=[(vx, v5), (sc, s1)], inc=vx)
        s2 = p.op("scalar", lambda e: e.activation(
            st4[:, 3:4], st4[:, 2:3], AF.Sqrt, bias=ctx.constf(1e-5)),
            waits=[(vx, v6)], inc=sc)
        v7 = p.op("vector", lambda e: e.reciprocal(st4[:, 3:4], st4[:, 3:4]),
                  waits=[(sc, s2)], inc=vx)
        v8 = p.op("vector", lambda e: e.scalar_tensor_tensor(
            st4[:, 4:5], st4[:, 1:2], -1.0, st4[:, 3:4], ALU.mult, ALU.mult),
            waits=[(vx, v7)], inc=vx)
        s3 = p.op("scalar", lambda e: e.activation(
            xnt[:], xt[:], AF.Identity, bias=st4[:, 4:5], scale=st4[:, 3:4]),
            waits=[(vx, v8), (st, st.n)], inc=sc)
        for c in range(NCH):
            t1 = p.op("tensor", lambda e, c=c: e.transpose(
                pst[:], xnt[:, c * 128:(c + 1) * 128], ctx.c["ident_f"][:]),
                waits=[(sc, s3), (vx, vx.n)], inc=mm)
            vv = p.op("vector", lambda e: e.tensor_copy(xbt[:], pst[:]),
                      waits=[(mm, t1)], inc=vx)
            p.op("sync", lambda e, c=c, i=i: e.dma_start(
                out=xn1FM[c][:, i * 128:(i + 1) * 128], in_=xbt[:]),
                waits=[(vx, vv)], inc=st, k=16)
            p.op("vector", lambda e: e.engine_nop(), waits=[(st, st.n)], inc=vx)
        s4 = p.op("scalar", lambda e: e.activation(
            sgt[:], xnt[:], AF.Square),
            waits=[(sc, s3), (vx, vx.n)], inc=sc)
        v8b = p.op("vector", lambda e: e.reduce_sum(st4[:, 5:6], sgt[:], axis=AX.X),
                   waits=[(sc, s4)], inc=vx)
        v9 = p.op("vector", lambda e: e.reduce_sum(st4[:, 6:7], xnt[:], axis=AX.X),
                  waits=[(sc, s3)], inc=vx)
        va = p.op("vector", lambda e: e.tensor_scalar(
            st4[:, 6:7], st4[:, 6:7], 1.0 / D, None, ALU.mult),
            waits=[(vx, v9)], inc=vx)
        vb = p.op("vector", lambda e: e.tensor_tensor(
            st4[:, 7:8], st4[:, 6:7], st4[:, 6:7], ALU.mult), waits=[(vx, va)], inc=vx)
        vc = p.op("vector", lambda e: e.scalar_tensor_tensor(
            st4[:, 7:8], st4[:, 5:6], 1.0 / D, st4[:, 7:8], ALU.mult, ALU.subtract),
            waits=[(vx, vb), (sc, s4)], inc=vx)
        s5 = p.op("scalar", lambda e: e.activation(
            st4[:, 8:9], st4[:, 7:8], AF.Sqrt, bias=ctx.constf(1e-3)),
            waits=[(vx, vc)], inc=sc)
        vd = p.op("vector", lambda e: e.reciprocal(st4[:, 8:9], st4[:, 8:9]),
                  waits=[(sc, s5)], inc=vx)
        ve = p.op("vector", lambda e: e.scalar_tensor_tensor(
            st4[:, 9:10], st4[:, 6:7], -1.0, st4[:, 8:9], ALU.mult, ALU.mult),
            waits=[(vx, vd)], inc=vx)
        s6 = p.op("scalar", lambda e: e.activation(
            xnt[:], xnt[:], AF.Identity, bias=st4[:, 9:10], scale=st4[:, 8:9]),
            waits=[(vx, ve), (mm, mm.n)], inc=sc)
        vf = p.op("vector", lambda e: e.scalar_tensor_tensor(
            xnt[:], xnt[:], 1.0, lngr[:], ALU.mult, ALU.mult),
            waits=[(sc, s6)], inc=vx)
        vg = p.op("vector", lambda e: e.tensor_add(xnt[:], xnt[:], lnbr[:]),
                  waits=[(vx, vf)], inc=vx)
        p.op("sync", lambda e, i=i: e.dma_start(
            out=AP(xnfD.tensor, i * 128 * D, [[D, 128], [1, D]]), in_=xnt[:]),
            waits=[(vx, vg)], inc=st, k=16)
        vh2 = p.op("vector", lambda e: e.reduce_max(
            mx[:, 1:2], xnt[:], axis=AX.X, apply_absolute_value=True),
            waits=[(vx, vg)], inc=vx)
        p.op("vector", lambda e: e.tensor_max(mx[:, 0:1], mx[:, 0:1], mx[:, 1:2]),
             waits=[(vx, vh2)], inc=vx)
    va2 = p.op("gpsimd", lambda e: e.dma_start(out=mxD, in_=mx[:, 0:1]),
               waits=[(vx, vx.n)], inc=lg, k=16)
    vb1 = p.op("gpsimd", lambda e: e.dma_start(
        out=m1o[0:1, 0:128], in_=mxD.rearrange("p o -> o p")),
        waits=[(lg, va2)], inc=lg, k=16)
    vb2 = p.op("vector", lambda e: e.reduce_max(
        m1o[0:1, 128:129], m1o[0:1, 0:128], axis=AX.X), waits=[(lg, vb1)], inc=vx)
    p.op("sync", lambda e: e.dma_start(out=m4_part, in_=m1o[0:1, 128:129]),
         waits=[(vx, vb2)], inc=st, k=16)
    p.drain(st)
    S.close()
    p.barrier()


def emit_l1attn(ctx, xn1FM, wqd_in, bqd_in, wqk_in, bqk_in, wkvd_in, bkvd_in,
                wvu_in, bvu_in, wproj_in, bproj_in, s1FM, dbg=None):
    import os as _os
    _esub = int(_os.environ.get("ESUB", "41"))
    p = ctx.p
    H1, KV, QL = 16, 64, 512
    S = ExitStack()
    wqdT = p.sb("a1_wqdT", [128, NCH * QL], BF16, scope=S)
    wkvT = p.sb("a1_wkvT", [128, NCH * KV], BF16, scope=S)
    wqkT = p.sb("a1_wqkT", [128, 4 * 1024], BF16, scope=S)
    wvuT = p.sb("a1_wvuT", [64, 1024], BF16, scope=S)
    wprT = p.sb("a1_wprT", [128, NCH * 1024], BF16, scope=S)
    wtmp = p.sb("a1_wtmp", [128, 1024], BF16, scope=S)
    bq = p.sb("a1_bq", [128, 4], F32, scope=S)
    bkv = p.sb("a1_bkv", [64, 1], F32, scope=S)
    bqk = p.sb("a1_bqk", [64, 16], F32, scope=S)
    bvur = p.sb("a1_bvur", [128, 1024], F32, scope=S)
    bpr = p.sb("a1_bpr", [128, NCH], F32, scope=S)
    brow = p.sb("a1_brow", [1, 1024], BF16, scope=S)
    cqa = p.sb("a1_cqa", [128, 4 * BT], BF16, scope=S)
    ckva = p.sb("a1_ckva", [64, BT], BF16, scope=S)
    psA = p.ps("a1_psA", [128, 512], F32, scope=S)
    Spst = ExitStack()
    pst = p.ps("a1_pst", [128, 128], BF16, scope=Spst)
    ld = p.sem()
    mm = p.sem()
    vx = p.sem()
    sc = p.sem()
    st = p.sem()

    def loadT(w_in, dst, R, C):
        for rc in range((R + 127) // 128):
            rr = min(128, R - rc * 128)
            lw = p.op("gpsimd", lambda e, rc=rc, rr=rr, w_in=w_in, C=C: e.dma_start(
                out=wtmp[0:rr, 0:C],
                in_=AP(w_in.tensor, rc * 128 * C, [[C, rr], [1, C]])),
                waits=[(mm, mm.n), (ld, ld.n)], inc=ld, k=16)
            for cc in range((C + 127) // 128):
                cw = min(128, C - cc * 128)
                t1 = p.op("tensor", lambda e, cc=cc, rr=rr, cw=cw: e.transpose(
                    pst[0:cw, 0:rr], wtmp[0:rr, cc * 128:cc * 128 + cw],
                    ctx.c["ident_bf"][0:rr, 0:rr]), waits=[(ld, lw), (vx, vx.n)],
                    inc=mm)
                p.op("vector", lambda e, cc=cc, rc=rc, rr=rr, cw=cw, dst=dst, R=R:
                     e.tensor_copy(dst[0:cw, cc * R + rc * 128:
                                       cc * R + rc * 128 + rr], pst[0:cw, 0:rr]),
                     waits=[(mm, t1)], inc=vx)

    loadT(wqd_in, wqdT, QL, 1024)
    loadT(wkvd_in, wkvT, KV, 1024)
    loadT(wqk_in, wqkT, 1024, QL)
    loadT(wvu_in, wvuT, 1024, KV)
    loadT(wproj_in, wprT, 1024, 1024)
    for cq in range(4):
        p.op("gpsimd", lambda e, cq=cq: e.dma_start(
            out=bq[:, cq:cq + 1], in_=AP(bqd_in.tensor, cq * 128, [[1, 128], [128, 1]])),
            inc=ld, k=16)
    p.op("gpsimd", lambda e: e.dma_start(
        out=bkv[:], in_=AP(bkvd_in.tensor, 0, [[1, 64], [64, 1]])), inc=ld, k=16)
    for h in range(H1):
        p.op("gpsimd", lambda e, h=h: e.dma_start(
            out=bqk[:, h:h + 1], in_=AP(bqk_in.tensor, h * 64, [[1, 64], [64, 1]])),
            inc=ld, k=16)
    for co in range(NCH):
        p.op("gpsimd", lambda e, co=co: e.dma_start(
            out=bpr[:, co:co + 1],
            in_=AP(bproj_in.tensor, co * 128, [[1, 128], [128, 1]])), inc=ld, k=16)
    lb1 = p.op("gpsimd", lambda e: e.dma_start(
        out=brow[:], in_=AP(bvu_in.tensor, 0, [[1024, 1], [1, 1024]])), inc=ld, k=16)
    for nn in range(2):
        t1 = p.op("tensor", lambda e, nn=nn: e.matmul(
            psA[:], ctx.c["ones_col"][:], brow[0:1, nn * 512:(nn + 1) * 512],
            start=True, stop=True), waits=[(ld, lb1), (vx, vx.n)], inc=mm)
        p.op("vector", lambda e, nn=nn: e.tensor_copy(
            bvur[:, nn * 512:(nn + 1) * 512], psA[:]), waits=[(mm, t1)], inc=vx)
    wl_ld, wl_vx = ld.n, vx.n

    Sx = ExitStack()
    xna = p.sb("a1_xna", [128, NCH * BT], BF16, scope=Sx)
    for c in range(NCH):
        p.op("sync", lambda e, c=c: e.dma_start(
            out=xna[:, c * BT:(c + 1) * BT], in_=xn1FM[c]),
            waits=[(ld, wl_ld), (vx, wl_vx)], inc=ld, k=16)
    lx = ld.n
    for qc in range(4 if _esub >= 2 else 0):
        for n in range(NCH):
            for c in range(NCH):
                p.op("tensor", lambda e, qc=qc, n=n, c=c: e.matmul(
                    psA[:], wqdT[:, c * QL + qc * 128: c * QL + (qc + 1) * 128],
                    xna[:, c * BT + n * 512: c * BT + (n + 1) * 512],
                    start=(c == 0), stop=(c == NCH - 1)),
                    waits=([(ld, lx), (sc, sc.n)] if c == 0 else []), inc=mm)
            p.op("scalar", lambda e, qc=qc, n=n: e.activation(
                cqa[:, qc * BT + n * 512: qc * BT + (n + 1) * 512], psA[:],
                AF.Identity, bias=bq[:, qc:qc + 1]), waits=[(mm, mm.n)], inc=sc)
    for n in range(NCH if _esub >= 2 else 0):
        for c in range(NCH):
            p.op("tensor", lambda e, n=n, c=c: e.matmul(
                psA[0:KV, :], wkvT[:, c * KV:(c + 1) * KV],
                xna[:, c * BT + n * 512: c * BT + (n + 1) * 512],
                start=(c == 0), stop=(c == NCH - 1)),
                waits=([(sc, sc.n)] if c == 0 else []), inc=mm)
        p.op("scalar", lambda e, n=n: e.activation(
            ckva[:, n * 512:(n + 1) * 512], psA[0:KV, :], AF.Identity,
            bias=bkv[:]), waits=[(mm, mm.n)], inc=sc)
    if dbg is not None:
        sdb = p.sem()
        p.op("sync", lambda e: e.dma_start(out=dbg[0], in_=cqa[:]),
             waits=[(sc, sc.n)], inc=sdb, k=16)
        p.op("sync", lambda e: e.dma_start(out=dbg[1][0:64, :], in_=ckva[:]),
             inc=sdb, k=16)
        p.drain(sdb)
    p.op("tensor", lambda e: e.nop(), waits=[(sc, sc.n)])
    Sx.close()
    p.barrier()

    Spst.close()
    Sb = ExitStack()
    vaTM = p.sb("a1_vaTM", [128, NT * 1024], BF16, scope=Sb)
    vaos = [p.sb(f"a1_vaos{i}", [128, 65], BF16, scope=Sb) for i in range(2)]
    qh = p.sb("a1_qh", [64, L], BF16, scope=Sb)
    sqq = p.sb("a1_sqq", [64, L], BF16, scope=Sb)
    exsb = p.sb("a1_exsb", [128, 4 * 512], BF16, scope=Sb)
    ya = p.sb("a1_ya", [128, NCH * L], BF16, scope=Sb)
    yt = p.sb("a1_yt", [64, 512], BF16, scope=Sb)
    srow = p.sb("a1_srow", [1, 512], F32, scope=Sb)
    srowb = p.sb("a1_srowb", [1, 512], BF16, scope=Sb)
    gq = p.sb("a1_gq", [1, 2 * L], F32, scope=Sb)
    gsc = p.sb("a1_gsc", [1, 8], F32, scope=Sb)
    gscb = p.sb("a1_gscb", [1, 1], BF16, scope=Sb)
    negc = p.sb("a1_negc", [128, 1], F32, scope=Sb)
    psQf = p.ps("a1_psQ", [128, 512], F32, scope=Sb)
    psQ = psQf[0:64, :]
    psS = p.ps("a1_psS", [128, 512], F32, scope=Sb)
    psR = psA[0:1, 0:512]
    psB = p.ps("a1_psB", [64, 512], F32, scope=Sb)
    psPV = [p.ps(f"a1_psPV{i}", [65, 512], F32, scope=Sb) for i in range(4)]
    for i in range(2):
        p.op("gpsimd", lambda e, i=i: e.memset(vaos[i][:, 64:65], 1.0), inc=vx)

    for b in range(B if _esub >= 3 else 0):
        bsl = slice(b * L, (b + 1) * L)
        for sct in range(NT):
            for nn in range(2):
                t1 = p.op("tensor", lambda e, sct=sct, nn=nn, b=b: e.matmul(
                    psA[:], ckva[:, b * L + sct * 128: b * L + (sct + 1) * 128],
                    wvuT[:, nn * 512:(nn + 1) * 512], start=True, stop=True),
                    waits=[(vx, vx.n), (mm, mm.n)], inc=mm)
                vv = p.op("vector", lambda e, sct=sct, nn=nn: e.scalar_tensor_tensor(
                    vaTM[:, sct * 1024 + nn * 512: sct * 1024 + (nn + 1) * 512],
                    psA[:], 1.0, bvur[:, nn * 512:(nn + 1) * 512],
                    ALU.mult, ALU.add), waits=[(mm, t1)], inc=vx)
        s1 = p.op("scalar", lambda e, b=b: e.activation(
            sqq[:, 0:L], ckva[:, bsl], AF.Square), waits=[(sc, sc.n)], inc=sc)
        for nn in range(4):
            t1 = p.op("tensor", lambda e, nn=nn: e.matmul(
                psR, ctx.c["ones128"][0:64, :], sqq[:, nn * 512:(nn + 1) * 512],
                start=True, stop=True), waits=[(sc, s1), (vx, vx.n)], inc=mm)
            p.op("vector", lambda e, nn=nn: e.tensor_copy(
                gq[0:1, nn * 512:(nn + 1) * 512], psR), waits=[(mm, t1)], inc=vx)
        vkv = p.op("vector", lambda e: e.reduce_max(
            gsc[0:1, 0:1], gq[0:1, 0:L], axis=AX.X), waits=[(vx, vx.n)], inc=vx)

        import os as _os2
        _qh = int(_os2.environ.get("QH", "16"))
        _qn = int(_os2.environ.get("QN", "4"))
        _qb = int(_os2.environ.get("QB_", "1"))
        for h in range(min(H1, _qh) if _esub >= 35 else 0):
            for n in range(_qn):
                for qc in range(4):
                    p.op("tensor", lambda e, h=h, n=n, qc=qc, b=b: e.matmul(
                        psQ, wqkT[:, qc * 1024 + h * 64: qc * 1024 + (h + 1) * 64],
                        cqa[:, qc * BT + b * L + n * 512: qc * BT + b * L + (n + 1) * 512],
                        start=(qc == 0), stop=(qc == 3)),
                        waits=([(sc, sc.n), (vx, vx.n)] if qc == 0 else []), inc=mm)
                if _qb:
                    p.op("scalar", lambda e, h=h, n=n: e.activation(
                        qh[:, n * 512:(n + 1) * 512], psQ, AF.Identity,
                        bias=bqk[:, h:h + 1]), waits=[(mm, mm.n)], inc=sc)
                else:
                    p.op("scalar", lambda e, h=h, n=n: e.activation(
                        qh[:, n * 512:(n + 1) * 512], psQ, AF.Identity),
                        waits=[(mm, mm.n)], inc=sc)
            if _esub < 37:
                continue
            s2 = p.op("scalar", lambda e: e.activation(
                sqq[:, 0:L], qh[:, 0:L], AF.Square), waits=[(sc, sc.n)], inc=sc)
            for nn in range(4):
                t1 = p.op("tensor", lambda e, nn=nn: e.matmul(
                    psR, ctx.c["ones128"][0:64, :], sqq[:, nn * 512:(nn + 1) * 512],
                    start=True, stop=True), waits=[(sc, s2), (vx, vx.n)], inc=mm)
                p.op("vector", lambda e, nn=nn: e.tensor_copy(
                    gq[0:1, nn * 512:(nn + 1) * 512], psR), waits=[(mm, t1)], inc=vx)
            vq = p.op("vector", lambda e: e.reduce_max(
                gsc[0:1, 1:2], gq[0:1, 0:L], axis=AX.X), waits=[(vx, vx.n)], inc=vx)
            v2 = p.op("vector", lambda e: e.tensor_tensor(
                gsc[0:1, 2:3], gsc[0:1, 1:2], gsc[0:1, 0:1], ALU.mult),
                waits=[(vx, vq)], inc=vx)
            s3 = p.op("scalar", lambda e: e.activation(
                gsc[0:1, 3:4], gsc[0:1, 2:3], AF.Sqrt, bias=ctx.constf(0.0, 1)),
                waits=[(vx, v2)], inc=sc)
            v3 = p.op("vector", lambda e: e.tensor_scalar(
                gsc[0:1, 4:5], gsc[0:1, 3:4], -0.125, None, ALU.mult),
                waits=[(sc, s3)], inc=vx)
            v4 = p.op("vector", lambda e: e.tensor_copy(gscb[:], gsc[0:1, 4:5]),
                      waits=[(vx, v3)], inc=vx)
            t2 = p.op("tensor", lambda e: e.matmul(
                psA[:, 0:1], ctx.c["ones_col"][:], gscb[:], start=True, stop=True),
                waits=[(vx, v4)], inc=mm)
            v5 = p.op("vector", lambda e: e.tensor_copy(negc[:], psA[:, 0:1]),
                      waits=[(mm, t2)], inc=vx)
            if _esub < 38:
                continue
            for sct in range(NT):
                io = sct % 2
                for nq in range(4):
                    t3 = p.op("tensor", lambda e, sct=sct, nq=nq, b=b: e.matmul(
                        psS[:], ckva[:, b * L + sct * 128: b * L + (sct + 1) * 128],
                        qh[:, nq * 512:(nq + 1) * 512], start=True, stop=True),
                        waits=[(sc, sc.n), (vx, v5 if sct == 0 and nq == 0 else vx.n)],
                        inc=mm)
                    p.op("scalar", lambda e, nq=nq: e.activation(
                        exsb[:, nq * 512:(nq + 1) * 512], psS[:], AF.Exp,
                        scale=0.125, bias=negc[:]), waits=[(mm, t3)], inc=sc)
                vo = p.op("vector", lambda e, sct=sct, h=h, io=io: e.tensor_copy(
                    vaos[io][:, 0:64], vaTM[:, sct * 1024 + h * 64: sct * 1024 + (h + 1) * 64]),
                    waits=[(mm, mm.n)], inc=vx)
                for nq in range(4 if _esub >= 39 else 0):
                    p.op("tensor", lambda e, sct=sct, nq=nq, io=io: e.matmul(
                        psPV[nq][:], vaos[io][:], exsb[:, nq * 512:(nq + 1) * 512],
                        start=(sct == 0), stop=(sct == NT - 1)),
                        waits=([(sc, sc.n), (vx, vo)] if nq == 0 else []), inc=mm)
            co, half = h // 2, (h % 2) * 64
            for nq in range(4 if _esub >= 40 else 0):
                se = p.op("scalar", lambda e, nq=nq: e.activation(
                    yt[:], psPV[nq][0:64, :], AF.Identity),
                    waits=[(mm, mm.n), (vx, vx.n)], inc=sc)
                v6 = p.op("vector", lambda e, nq=nq: e.tensor_copy(
                    srow[:], psPV[nq][64:65, :]), waits=[(mm, mm.n)], inc=vx)
                v7 = p.op("vector", lambda e: e.reciprocal(srow[:], srow[:]),
                          waits=[(vx, v6)], inc=vx)
                v8 = p.op("vector", lambda e: e.tensor_copy(srowb[:], srow[:]),
                          waits=[(vx, v7)], inc=vx)
                t4 = p.op("tensor", lambda e: e.matmul(
                    psB[:], ctx.c["ones_col"][0:1, 0:64], srowb[:],
                    start=True, stop=True), waits=[(vx, v8)], inc=mm)
                p.op("vector", lambda e, co=co, half=half, nq=nq: e.tensor_tensor(
                    ya[half:half + 64, co * L + nq * 512: co * L + (nq + 1) * 512],
                    yt[:], psB[:], ALU.mult), waits=[(mm, t4), (sc, se)], inc=vx)
        _ep = int(_os.environ.get("EPROJ", "3"))
        for co in range(NCH if _esub >= 41 else 0):
            for n in range(4):
                for ci in range(NCH):
                    p.op("tensor", lambda e, co=co, n=n, ci=ci: e.matmul(
                        psS[:], wprT[:, ci * 1024 + co * 128: ci * 1024 + (co + 1) * 128],
                        ya[:, ci * L + n * 512: ci * L + (n + 1) * 512],
                        start=(ci == 0), stop=(ci == NCH - 1)),
                        waits=([(vx, vx.n), (sc, sc.n)] if ci == 0 else []), inc=mm)
                if _ep >= 2:
                    s9 = p.op("scalar", lambda e, co=co: e.activation(
                        exsb[:, 0:512], psS[:],
                        AF.Identity, bias=bpr[:, co:co + 1]), waits=[(mm, mm.n)], inc=sc)
                else:
                    s9 = p.op("vector", lambda e: e.tensor_copy(
                        exsb[:, 0:512], psS[:]), waits=[(mm, mm.n)], inc=vx)
                    sc.n += 0
                if _ep >= 3:
                    p.op("sync", lambda e, co=co, n=n, b=b: e.dma_start(
                        out=s1FM[co][:, b * L + n * 512: b * L + (n + 1) * 512],
                        in_=exsb[:, 0:512]),
                        waits=([(sc, s9)] if _ep >= 2 else [(vx, s9)]), inc=st, k=16)
                    p.op("scalar", lambda e: e.nop(), waits=[(st, st.n)], inc=sc)
    p.drain(st)
    Sb.close()
    S.close()
    p.barrier()


def emit_l1ffn(ctx, x1_in, s1FM, m4v_in, w_in, out_final, dbg=None):
    p = ctx.p
    S = ExitStack()
    wqb = p.sb("f1_wq", [128, NCH * 1024], BF16, scope=S)
    wt = p.sb("f1_wt", [128, 1024], F32, scope=S)
    wi = p.sb("f1_wi", [128, 1024], I32, scope=S)
    asum = p.sb("f1_asum", [128, NCH + 1], F32, scope=S)
    arow = p.sb("f1_arow", [1, 130], F32, scope=S)
    rep = p.sb("f1_rep", [128, 2], F32, scope=S)
    glb = p.sb("f1_glb", [1, 1], BF16, scope=S)
    xq1a = p.sb("f1_xq1a", [128, NCH * BT], BF16, scope=S)
    psA = p.ps("f1_psA", [128, 512], F32, scope=S)
    mxD = ctx.d["scr128"]
    ld = p.sem()
    lg = p.sem()
    mm = p.sem()
    vx = p.sem()
    sc = p.sem()
    st = p.sem()

    for dc in range(NCH):
        l1 = p.op("sync", lambda e, dc=dc: e.dma_start(
            out=wt[:], in_=AP(w_in.tensor, dc * 128 * 1024, [[1024, 128], [1, 1024]])),
            waits=[(vx, vx.n)], inc=ld, k=16)
        p.op("vector", lambda e, dc=dc: e.reduce_sum(
            asum[:, dc:dc + 1], wt[:], axis=AX.X, apply_absolute_value=True),
            waits=[(ld, l1)], inc=vx)
    vs = p.op("vector", lambda e: e.reduce_sum(
        asum[:, NCH:NCH + 1], asum[:, 0:NCH], axis=AX.X),
        waits=[(vx, vx.n)], inc=vx)
    l2 = p.op("gpsimd", lambda e: e.dma_start(out=mxD, in_=asum[:, NCH:NCH + 1]),
              waits=[(vx, vs)], inc=lg, k=16)
    l3 = p.op("gpsimd", lambda e: e.dma_start(
        out=arow[0:1, 0:128], in_=mxD.rearrange("p o -> o p")),
        waits=[(lg, l2)], inc=lg, k=16)
    v1 = p.op("vector", lambda e: e.reduce_sum(
        arow[0:1, 128:129], arow[0:1, 0:128], axis=AX.X), waits=[(lg, l3)], inc=vx)
    v2 = p.op("vector", lambda e: e.tensor_scalar(
        arow[0:1, 128:129], arow[0:1, 128:129], 1.0 / (1024.0 * 1024.0), 1e-5,
        ALU.mult, ALU.add), waits=[(vx, v1)], inc=vx)
    v3 = p.op("vector", lambda e: e.reciprocal(arow[0:1, 129:130], arow[0:1, 128:129]),
              waits=[(vx, v2)], inc=vx)
    v4 = p.op("vector", lambda e: e.tensor_scalar(
        arow[0:1, 127:128], arow[0:1, 128:129], 1.0, -1e-5,
        ALU.mult, ALU.add), waits=[(vx, v3)], inc=vx)
    v5 = p.op("vector", lambda e: e.tensor_copy(glb[:], arow[0:1, 129:130]),
              waits=[(vx, v4)], inc=vx)
    t1 = p.op("tensor", lambda e: e.matmul(psA[:, 0:1], ctx.c["ones_col"][:], glb[:],
                                           start=True, stop=True),
              waits=[(vx, v5)], inc=mm)
    v6 = p.op("vector", lambda e: e.tensor_copy(rep[:, 0:1], psA[:, 0:1]),
              waits=[(mm, t1)], inc=vx)
    for dc in range(NCH):
        l4 = p.op("sync", lambda e, dc=dc: e.dma_start(
            out=wt[:], in_=AP(w_in.tensor, dc * 128 * 1024, [[1024, 128], [1, 1024]])),
            waits=[(vx, vx.n)], inc=ld, k=16)
        q1 = p.op("vector", lambda e: e.tensor_scalar(
            wi[:], wt[:], rep[:, 0:1], None, ALU.mult), waits=[(ld, l4), (vx, v6)],
            inc=vx)
        q1b = p.op("vector", lambda e: e.tensor_copy(wt[:], wi[:]),
                   waits=[(vx, q1)], inc=vx)
        q2 = p.op("vector", lambda e: e.tensor_scalar(
            wt[:], wt[:], -1.0, 1.0, ALU.max, ALU.min),
            waits=[(vx, q1b)], inc=vx)
        q3 = p.op("vector", lambda e, dc=dc: e.tensor_copy(
            wqb[:, dc * 1024:(dc + 1) * 1024], wt[:]), waits=[(vx, q2)], inc=vx)
        if dbg is not None and dc == 0:
            sdb0 = p.sem()
            p.op("sync", lambda e: e.dma_start(out=dbg[2], in_=wi[:]),
                 waits=[(vx, q3)], inc=sdb0, k=16)
            p.op("sync", lambda e: e.dma_start(out=dbg[3], in_=wt[:]),
                 inc=sdb0, k=16)
            p.op("sync", lambda e: e.dma_start(out=dbg[4], in_=wqb[:, 0:1024]),
                 inc=sdb0, k=16)
            p.drain(sdb0)
    wq_done = vx.n
    if dbg is not None:
        sdb = p.sem()
        p.op("sync", lambda e: e.dma_start(out=dbg[0], in_=wqb[:]),
             waits=[(vx, wq_done)], inc=sdb, k=16)
        p.op("sync", lambda e: e.dma_start(out=dbg[1], in_=arow[:]),
             waits=[(vx, wq_done)], inc=sdb, k=16)
        p.drain(sdb)

    l5 = p.op("gpsimd", lambda e: e.dma_start(
        out=arow[0:1, 0:8], in_=AP(m4v_in.tensor, 0, [[8, 1], [1, 8]])),
        waits=[(vx, vx.n)], inc=lg, k=16)
    v7 = p.op("vector", lambda e: e.reduce_max(
        arow[0:1, 8:9], arow[0:1, 0:8], axis=AX.X), waits=[(lg, l5)], inc=vx)
    v8 = p.op("vector", lambda e: e.reciprocal(arow[0:1, 9:10], arow[0:1, 8:9]),
              waits=[(vx, v7)], inc=vx)
    v9 = p.op("vector", lambda e: e.tensor_scalar(
        arow[0:1, 9:10], arow[0:1, 9:10], 128.0, None, ALU.mult),
        waits=[(vx, v8)], inc=vx)
    va = p.op("vector", lambda e: e.tensor_tensor(
        arow[0:1, 10:11], arow[0:1, 8:9], arow[0:1, 127:128], ALU.mult),
        waits=[(vx, v9)], inc=vx)
    vb = p.op("vector", lambda e: e.tensor_scalar(
        arow[0:1, 10:11], arow[0:1, 10:11], 1.0 / 128.0, None, ALU.mult),
        waits=[(vx, va)], inc=vx)
    vc = p.op("vector", lambda e: e.tensor_copy(glb[:], arow[0:1, 9:10]),
              waits=[(vx, vb)], inc=vx)
    t2 = p.op("tensor", lambda e: e.matmul(psA[:, 0:1], ctx.c["ones_col"][:], glb[:],
                                           start=True, stop=True),
              waits=[(vx, vc)], inc=mm)
    vd = p.op("vector", lambda e: e.tensor_copy(rep[:, 0:1], psA[:, 0:1]),
              waits=[(mm, t2)], inc=vx)
    ve = p.op("vector", lambda e: e.tensor_copy(glb[:], arow[0:1, 10:11]),
              waits=[(vx, vd)], inc=vx)
    t3 = p.op("tensor", lambda e: e.matmul(psA[:, 0:1], ctx.c["ones_col"][:], glb[:],
                                           start=True, stop=True),
              waits=[(vx, ve)], inc=mm)
    vf = p.op("vector", lambda e: e.tensor_copy(rep[:, 1:2], psA[:, 0:1]),
              waits=[(mm, t3)], inc=vx)
    scale_done = vf

    xnfD = ctx.d["xnfD"]
    Sx = ExitStack()
    xnt = p.sb("f1_xnt", [128, D], F32, scope=Sx)
    xbt = p.sb("f1_xbt", [128, D], BF16, scope=Sx)
    pst = p.ps("f1_pst", [128, 128], BF16, scope=Sx)
    for i in range(NTT):
        li = p.op("sync", lambda e, i=i: e.dma_start(
            out=xnt[:], in_=AP(xnfD.tensor, i * 128 * D, [[D, 128], [1, D]])),
            waits=[(vx, vx.n), (mm, mm.n)], inc=ld, k=16)
        qq = p.op("vector", lambda e: e.tensor_scalar(
            xbt[:], xnt[:], rep[:, 0:1], None, ALU.mult),
            waits=[(ld, li), (vx, scale_done)], inc=vx)
        q2 = p.op("vector", lambda e: e.tensor_scalar(
            xbt[:], xbt[:], -128.0 + 1e-5, 128.0 - 1e-5, ALU.max, ALU.min),
            waits=[(vx, qq)], inc=vx)
        for c in range(NCH):
            tt = p.op("tensor", lambda e, c=c: e.transpose(
                pst[:], xbt[:, c * 128:(c + 1) * 128], ctx.c["ident_bf"][:]),
                waits=[(vx, vx.n)], inc=mm)
            p.op("vector", lambda e, c=c, i=i: e.tensor_copy(
                xq1a[:, c * BT + i * 128: c * BT + (i + 1) * 128], pst[:]),
                waits=[(mm, tt)], inc=vx)
    Sx.close()

    Sy = ExitStack()
    s1b = p.sb("f1_s1b", [128, D], BF16, scope=Sy)
    gb = p.sb("f1_gb", [128, D], BF16, scope=Sy)
    xt = p.sb("f1_xt", [128, D], F32, scope=Sy)
    sgt = p.sb("f1_sgt", [128, D], F32, scope=Sy)
    pst2 = p.ps("f1_pst2", [128, 128], BF16, scope=Sy)
    x1_t = x1_in.rearrange("b (jt p) d -> (b jt) p d", p=128)
    out_t = out_final.rearrange("b (jt p) d -> (b jt) p d", p=128)
    gFMl = ctx.d["gFM1"]
    for co in range(NCH):
        for n in range(NCH):
            for ci in range(NCH):
                p.op("tensor", lambda e, co=co, n=n, ci=ci: e.matmul(
                    psA[:], wqb[:, ci * 1024 + co * 128: ci * 1024 + (co + 1) * 128],
                    xq1a[:, ci * BT + n * 512: ci * BT + (n + 1) * 512],
                    start=(ci == 0), stop=(ci == NCH - 1)),
                    waits=([(vx, vx.n), (sc, sc.n)] if ci == 0 else []), inc=mm)
            s1_ = p.op("scalar", lambda e: e.activation(
                gb[:, 0:512], psA[:], AF.Identity, scale=rep[:, 1:2]),
                waits=[(mm, mm.n)], inc=sc)
            s2_ = p.op("scalar", lambda e: e.activation(
                gb[:, 0:512], gb[:, 0:512], AF.Silu), waits=[(sc, s1_)], inc=sc)
            p.op("sync", lambda e, co=co, n=n, s2_=s2_: e.dma_start(
                out=gFMl[co][:, n * 512:(n + 1) * 512], in_=gb[:, 0:512]),
                waits=[(sc, s2_)], inc=st, k=16)
            p.op("scalar", lambda e: e.nop(), waits=[(st, st.n)], inc=sc)
    gdone = st.n
    for i in range(NTT):
        b, jt = i // NT, i % NT
        lx = p.op("sync", lambda e, i=i: e.dma_start(out=xt[:], in_=x1_t[i]),
                  waits=[(vx, vx.n), (st, st.n)], inc=ld, k=16)
        for c in range(NCH):
            p.op("sync", lambda e, c=c, b=b, jt=jt: e.dma_start(
                out=s1b[:, c * 128:(c + 1) * 128],
                in_=s1FM[c][:, b * L + jt * 128: b * L + (jt + 1) * 128]),
                waits=([(mm, mm.n), (vx, vx.n)] if c == 0 else []), inc=ld, k=16)
            p.op("sync", lambda e, c=c, b=b, jt=jt: e.dma_start(
                out=gb[:, c * 128:(c + 1) * 128],
                in_=gFMl[c][:, b * L + jt * 128: b * L + (jt + 1) * 128]), inc=ld, k=16)
        lsg = ld.n
        v1_ = p.op("vector", lambda e: e.tensor_mul(s1b[:], s1b[:], gb[:]),
                   waits=[(ld, lsg)], inc=vx)
        for c in range(NCH):
            t1_ = p.op("tensor", lambda e, c=c: e.transpose(
                pst2[:], s1b[:, c * 128:(c + 1) * 128], ctx.c["ident_bf"][:]),
                waits=[(vx, vx.n)], inc=mm)
            p.op("vector", lambda e, c=c: e.tensor_copy(
                sgt[:, c * 128:(c + 1) * 128], pst2[:]), waits=[(mm, t1_)], inc=vx)
        v2_ = p.op("vector", lambda e: e.tensor_add(xt[:], xt[:], sgt[:]),
                   waits=[(vx, vx.n), (ld, lx)], inc=vx)
        p.op("sync", lambda e, i=i: e.dma_start(out=out_t[i], in_=xt[:]),
             waits=[(vx, v2_)], inc=st, k=16)
        p.op("vector", lambda e: e.engine_nop(), waits=[(st, st.n)], inc=vx)
    p.drain(st)
    Sy.close()
    S.close()
    p.barrier()


def _mk_prog():
    nc = bass.Bass("TRN2", target_bir_lowering=False, debug=False)
    stack = ExitStack()
    p = Prog(nc, stack)
    ctx = Ctx(nc, p)
    ctx.d["scr128"] = nc.dram_tensor("scr128", [128, 1], F32, kind="Internal").ap()
    return nc, stack, p, ctx


def _fmo(nc, name, kind="Internal"):
    t = nc.dram_tensor(name, [NCH, 128, BT], BF16, kind=kind).ap()
    return t, [t[c] for c in range(NCH)]


def build_A():
    nc, stack, p, ctx = _mk_prog()
    I = lambda n, s: nc.dram_tensor(n, s, F32, kind="ExternalInput").ap()
    x = I("x", [B, L, D])
    wq, wk, wv = I("wq", [D, D]), I("wk", [D, D]), I("wv", [D, D])
    wb, crl = I("wb", [D, H]), I("crl", [H])
    cwq, cwk, cwv = I("cwq", [D, 4]), I("cwk", [D, 4]), I("cwv", [D, 4])
    fsw, flw = I("fsw", [D, 5]), I("flw", [D, 64])
    w1, b1 = I("w1", [1040, 1024]), I("b1", [1024])
    w2, b2 = I("w2", [1024, 4]), I("b2", [4])
    lt, rmw, wo = I("lt", [1]), I("rmw", [256]), I("wo", [D, D])
    m1 = nc.dram_tensor("m1", [1, 1], F32, kind="ExternalOutput").ap()
    sFM_t, sFM = _fmo(nc, "sFM", "ExternalOutput")
    _, xnFM = _fmo(nc, "xnFM")
    _, qFM = _fmo(nc, "qFM")
    _, kFM = _fmo(nc, "kFM")
    _, vFM = _fmo(nc, "vFM")
    _, lsFM = _fmo(nc, "lsFM")
    _, llFM = _fmo(nc, "llFM")
    _, doFM = _fmo(nc, "doFM")
    _, oFM = _fmo(nc, "oFM")
    doTM = nc.dram_tensor("doTM", [B, H, NT, 128, 256], BF16, kind="Internal").ap()
    TB = nc.dram_tensor("TB", [D, 192, 128], BF16, kind="Internal").ap()
    wpad = nc.dram_tensor("wpad", [D, 320], BF16, kind="Internal").ap()
    statsD = nc.dram_tensor("statsD", [16, 4, BT], BF16, kind="Internal").ap()
    probsD = nc.dram_tensor("probsD", [4, 4, BT], BF16, kind="Internal").ap()

    emit_consts(ctx)
    emit_hmasks(ctx)
    emit_ln0(ctx, x, xnFM, m1)
    ctx.epoch = ExitStack()
    emit_beta(ctx, wb, crl)
    emit_qkv(ctx, (wq, wk, wv), (cwq, cwk, cwv), (qFM, kFM, vFM))
    emit_delta(ctx, qFM, kFM, vFM, doTM)
    ctx.epoch.close()
    emit_dwconv_diag(ctx, vFM, fsw, 5, lsFM)
    emit_do_fm(ctx, doTM, doFM)
    emit_fir_long(ctx, vFM, flw, llFM, TB, wpad)
    brFM = dict(ls=lsFM, ll=llFM, do=doFM, vh=vFM)
    emit_mix(ctx, brFM, w1, b1, w2, b2, lt, rmw, wo, sFM, oFM, statsD, probsD)
    p.build()
    stack.close()
    return nc


def build_B(phase):
    nc, stack, p, ctx = _mk_prog()
    I = lambda n, s: nc.dram_tensor(n, s, F32, kind="ExternalInput").ap()
    x = I("x", [B, L, D])
    m1v = I("m1v", [8])
    clip_x = I("clip_x", [1])
    wg, wu = I("wg", [2816, 1024]), I("wu", [2816, 1024])
    wgT = nc.dram_tensor("wgT", [NCH, 128, 2816], BF16, kind="Internal").ap()
    wuT = nc.dram_tensor("wuT", [NCH, 128, 2816], BF16, kind="Internal").ap()
    if phase == "b":
        m2v = I("m2v", [8])
        clip_d = I("clip_d", [1])
        wd = I("wd", [1024, 2816])
        wdT = nc.dram_tensor("wdT", [22, 128, 1024], BF16, kind="Internal").ap()
        gFM_t, gFM = _fmo(nc, "gFM", "ExternalOutput")
        m2 = None
    else:
        m2v = clip_d = wd = wdT = gFM = None
        m2 = nc.dram_tensor("m2", [1, 1], F32, kind="ExternalOutput").ap()

    import os as _os
    _sub = int(_os.environ.get("BSUB", "3"))
    emit_consts(ctx)
    xqa = p.sb("xqa", [128, NCH * BT], BF16)
    emit_ln_quant(ctx, x, m1v, clip_x, xqa)
    if _sub >= 2:
        emit_wquant_gu(ctx, wg, wgT)
        emit_wquant_gu(ctx, wu, wuT)
        if phase == "b":
            emit_wquant_gu(ctx, wd, wdT)
    if _sub >= 3:
        emit_ffn0(ctx, xqa, wgT, wuT, wdT, m2v, clip_d, gFM, m2, phase)
    elif phase == "a":
        st0 = p.sem()
        zz = p.sb("zz", [1, 1], F32)
        p.op("vector", lambda e: e.memset(zz[:], 0.0), inc=st0)
        p.op("sync", lambda e: e.dma_start(out=m2, in_=zz[:]),
             waits=[(st0, st0.n)], inc=st0, k=16)
        p.drain(st0)
    p.build()
    stack.close()
    return nc


def build_D():
    nc, stack, p, ctx = _mk_prog()
    I = lambda n, s: nc.dram_tensor(n, s, F32, kind="ExternalInput").ap()
    x = I("x", [B, L, D])
    lng, lnb = I("lng", [D]), I("lnb", [D])
    sFMi = nc.dram_tensor("sFMi", [NCH, 128, BT], BF16, kind="ExternalInput").ap()
    gFMi = nc.dram_tensor("gFMi", [NCH, 128, BT], BF16, kind="ExternalInput").ap()
    x1 = nc.dram_tensor("x1", [B, L, D], F32, kind="ExternalOutput").ap()
    xn1FM_t, xn1FM = _fmo(nc, "xn1FM", "ExternalOutput")
    m4 = nc.dram_tensor("m4", [1, 1], F32, kind="ExternalOutput").ap()
    xnfD = nc.dram_tensor("xnfD", [NTT, 128, D], F32, kind="ExternalOutput").ap()
    emit_consts(ctx)
    emit_combine_ln1(ctx, x, [sFMi[c] for c in range(NCH)],
                     [gFMi[c] for c in range(NCH)], x1, xn1FM, m4, lng, lnb, xnfD)
    p.build()
    stack.close()
    return nc


def build_E():
    nc, stack, p, ctx = _mk_prog()
    I = lambda n, s: nc.dram_tensor(n, s, F32, kind="ExternalInput").ap()
    xn1FMi = nc.dram_tensor("xn1FMi", [NCH, 128, BT], BF16, kind="ExternalInput").ap()
    wqd, bqd = I("wqd", [512, 1024]), I("bqd", [512])
    wqk, bqk = I("wqk", [1024, 512]), I("bqk", [1024])
    wkvd, bkvd = I("wkvd", [64, 1024]), I("bkvd", [64])
    wvu, bvu = I("wvu", [1024, 64]), I("bvu", [1024])
    wpr, bpr = I("wpr", [1024, 1024]), I("bpr", [1024])
    s1FM_t, s1FM = _fmo(nc, "s1FM", "ExternalOutput")
    import os as _os
    dbg = None
    if _os.environ.get("EDBG", "0") == "1":
        d0 = nc.dram_tensor("dbg_cqa", [128, 4 * BT], BF16, kind="ExternalOutput").ap()
        d1 = nc.dram_tensor("dbg_ckva", [128, BT], BF16, kind="ExternalOutput").ap()
        dbg = [d0, d1]
    emit_consts(ctx)
    ones128 = p.sb("ones128", [128, 1], BF16)
    p.op("gpsimd", lambda e: e.memset(ones128[:], 1.0))
    ctx.c["ones128"] = ones128
    p.barrier()
    emit_l1attn(ctx, [xn1FMi[c] for c in range(NCH)], wqd, bqd, wqk, bqk,
                wkvd, bkvd, wvu, bvu, wpr, bpr, s1FM, dbg=dbg)
    p.build()
    stack.close()
    return nc


def build_CDE():
    nc, stack, p, ctx = _mk_prog()
    I = lambda n, s: nc.dram_tensor(n, s, F32, kind="ExternalInput").ap()
    x = I("x", [B, L, D])
    m1v = I("m1v", [8])
    clip_x = I("clip_x", [1])
    wg, wu = I("wg", [2816, 1024]), I("wu", [2816, 1024])
    m2v = I("m2v", [8])
    clip_d = I("clip_d", [1])
    wd = I("wd", [1024, 2816])
    lng, lnb = I("lng", [D]), I("lnb", [D])
    sFMi = nc.dram_tensor("sFMi", [NCH, 128, BT], BF16, kind="ExternalInput").ap()
    wqd, bqd = I("wqd", [512, 1024]), I("bqd", [512])
    wqk, bqk = I("wqk", [1024, 512]), I("bqk", [1024])
    wkvd, bkvd = I("wkvd", [64, 1024]), I("bkvd", [64])
    wvu, bvu = I("wvu", [1024, 64]), I("bvu", [1024])
    wpr, bpr = I("wpr", [1024, 1024]), I("bpr", [1024])
    wgT = nc.dram_tensor("wgT", [NCH, 128, 2816], BF16, kind="Internal").ap()
    wuT = nc.dram_tensor("wuT", [NCH, 128, 2816], BF16, kind="Internal").ap()
    wdT = nc.dram_tensor("wdT", [22, 128, 1024], BF16, kind="Internal").ap()
    gFM_t, gFM = _fmo(nc, "gFM", "Internal")
    xn1FM_t, xn1FM = _fmo(nc, "xn1FM", "Internal")
    x1 = nc.dram_tensor("x1", [B, L, D], F32, kind="ExternalOutput").ap()
    m4 = nc.dram_tensor("m4", [1, 1], F32, kind="ExternalOutput").ap()
    xnfD = nc.dram_tensor("xnfD", [NTT, 128, D], F32, kind="ExternalOutput").ap()
    s1FM_t, s1FM = _fmo(nc, "s1FM", "ExternalOutput")

    emit_consts(ctx)
    ones128 = p.sb("ones128", [128, 1], BF16)
    p.op("gpsimd", lambda e: e.memset(ones128[:], 1.0))
    ctx.c["ones128"] = ones128
    p.barrier()
    Sx = ExitStack()
    xqa = p.sb("xqa", [128, NCH * BT], BF16, scope=Sx)
    emit_ln_quant(ctx, x, m1v, clip_x, xqa)
    emit_wquant_gu(ctx, wg, wgT)
    emit_wquant_gu(ctx, wu, wuT)
    emit_wquant_gu(ctx, wd, wdT)
    emit_ffn0(ctx, xqa, wgT, wuT, wdT, m2v, clip_d, gFM, None, "b")
    Sx.close()
    emit_combine_ln1(ctx, x, [sFMi[c] for c in range(NCH)], gFM, x1, xn1FM, m4,
                     lng, lnb, xnfD)
    emit_l1attn(ctx, xn1FM, wqd, bqd, wqk, bqk, wkvd, bkvd, wvu, bvu, wpr, bpr,
                s1FM)
    p.build()
    stack.close()
    return nc


def build_F():
    nc, stack, p, ctx = _mk_prog()
    I = lambda n, s: nc.dram_tensor(n, s, F32, kind="ExternalInput").ap()
    x1 = I("x1", [B, L, D])
    m4v = I("m4v", [8])
    w = I("w", [1024, 1024])
    xnfDi = nc.dram_tensor("xnfDi", [NTT, 128, D], F32, kind="ExternalInput").ap()
    s1FMi = nc.dram_tensor("s1FMi", [NCH, 128, BT], BF16, kind="ExternalInput").ap()
    outf = nc.dram_tensor("outf", [B, L, D], F32, kind="ExternalOutput").ap()
    import os as _os
    gkind = "ExternalOutput" if _os.environ.get("FDBG", "0") == "1" else "Internal"
    gFM1_t = nc.dram_tensor("gFM1", [NCH, 128, BT], BF16, kind=gkind).ap()
    ctx.d["gFM1"] = [gFM1_t[c] for c in range(NCH)]
    ctx.d["xnfD"] = xnfDi
    dbgf = None
    if _os.environ.get("FDBG", "0") == "1":
        dd0 = nc.dram_tensor("dbg_wqb", [128, NCH * 1024], BF16, kind="ExternalOutput").ap()
        dd1 = nc.dram_tensor("dbg_arow", [1, 130], F32, kind="ExternalOutput").ap()
        dd2 = nc.dram_tensor("dbg_wi", [128, 1024], I32, kind="ExternalOutput").ap()
        dd3 = nc.dram_tensor("dbg_wt", [128, 1024], F32, kind="ExternalOutput").ap()
        dd4 = nc.dram_tensor("dbg_wq0", [128, 1024], BF16, kind="ExternalOutput").ap()
        dbgf = [dd0, dd1, dd2, dd3, dd4]
    emit_consts(ctx)
    emit_l1ffn(ctx, x1, [s1FMi[c] for c in range(NCH)], m4v, w, outf, dbg=dbgf)
    p.build()
    stack.close()
    return nc


WA = dict(wq="L0_attn_Wq", wk="L0_attn_Wk", wv="L0_attn_Wv", wb="L0_attn_Wb",
          crl="L0_attn_conv_res_logit", cwq="L0_attn_conv_w_q",
          cwk="L0_attn_conv_w_k", cwv="L0_attn_conv_w_v",
          w1="L0_attn_mlp_w1", b1="L0_attn_mlp_b1", w2="L0_attn_mlp_w2",
          b2="L0_attn_mlp_b2", lt="L0_attn_logit_temp", rmw="L0_attn_rms_w",
          wo="L0_attn_Wo")


def kernel(**ins):
    NC = 8
    x = np.ascontiguousarray(ins["L0_attn_hidden_states"], dtype=np.float32)
    fsw = np.ascontiguousarray(
        np.asarray(ins["L0_attn_fir_short_w"], np.float32).reshape(D, 5)[:, ::-1])
    flw = np.ascontiguousarray(
        np.asarray(ins["L0_attn_fir_long_w"], np.float32).reshape(D, 64)[:, ::-1])
    f32 = lambda k: np.ascontiguousarray(np.asarray(ins[k], np.float32))
    cores = list(range(NC))

    ncA = build_A()
    imA = []
    for c in cores:
        m = {k: f32(v) for k, v in WA.items()}
        m["x"] = x[2 * c:2 * c + 2]
        m["fsw"] = fsw
        m["flw"] = flw
        imA.append(m)
    rA = run_bass_kernel_spmd(ncA, imA, core_ids=cores)
    m1v = np.concatenate([rA.results[c]["m1"].reshape(1) for c in cores])
    sFMs = [rA.results[c]["sFM"] for c in cores]

    ncB = build_B("a")
    imB = [{"x": x[2 * c:2 * c + 2], "m1v": m1v, "clip_x": f32("L0_ffn_clip_x"),
            "wg": f32("L0_ffn_w_gate"), "wu": f32("L0_ffn_w_up")} for c in cores]
    rB = run_bass_kernel_spmd(ncB, imB, core_ids=cores)
    m2v = np.concatenate([rB.results[c]["m2"].reshape(1) for c in cores])

    ncC = build_CDE()
    imC = [{"x": x[2 * c:2 * c + 2], "m1v": m1v, "clip_x": f32("L0_ffn_clip_x"),
            "m2v": m2v, "clip_d": f32("L0_ffn_clip_down"),
            "wg": f32("L0_ffn_w_gate"), "wu": f32("L0_ffn_w_up"),
            "wd": f32("L0_ffn_w_down"),
            "lng": f32("L1_ffn_ln_gamma"), "lnb": f32("L1_ffn_ln_beta"),
            "sFMi": sFMs[c],
            "wqd": f32("L1_attn_wq_d"), "bqd": f32("L1_attn_bq_d"),
            "wqk": f32("L1_attn_w_qk"), "bqk": f32("L1_attn_b_qk"),
            "wkvd": f32("L1_attn_wkv_d"), "bkvd": f32("L1_attn_bkv_d"),
            "wvu": f32("L1_attn_wv_u"), "bvu": f32("L1_attn_bv_u"),
            "wpr": f32("L1_attn_w_proj"), "bpr": f32("L1_attn_b_proj")}
           for c in cores]
    rD = run_bass_kernel_spmd(ncC, imC, core_ids=cores)
    m4v = np.concatenate([rD.results[c]["m4"].reshape(1) for c in cores])

    ncF = build_F()
    imF = [{"x1": rD.results[c]["x1"], "m4v": m4v, "w": f32("L1_ffn_weight"),
            "xnfDi": rD.results[c]["xnfD"], "s1FMi": rD.results[c]["s1FM"]}
           for c in cores]
    rF = run_bass_kernel_spmd(ncF, imF, core_ids=cores)
    out = np.concatenate([rF.results[c]["outf"] for c in cores], axis=0)
    return out
